# version 02
# baseline (speedup 1.0000x reference)
import numpy as np
import ml_dtypes
import concourse.bass as bass
import concourse.mybir as mybir
from concourse.bass_utils import run_bass_kernel_spmd

F32 = mybir.dt.float32
BF16 = mybir.dt.bfloat16
AF = mybir.ActivationFunctionType
ALU = mybir.AluOpType
AX = mybir.AxisListType

NCORE = 8
TC = 1024
D = 2048
DC = 16
H = 8
DK = 256
CH = 128
NCH = TC // CH
KCONV = 4
L2_EPS = 1e-6
RMS_EPS = 1e-5
LN_EPS = 1e-5
NE = 16
IE = 1408
ICN = IE // 128
SI = 2816
SICN = SI // 128

PTILES = [(0, 343), (343, 343), (686, 341)]

DBG = {"keys": (), "out": None, "phase": "H"}


class Ctr:
    __slots__ = ("h", "n", "name")

    def __init__(self, h, name):
        self.h, self.n, self.name = h, 0, name


class Prog:
    ENG = ("pe", "dve", "act", "pool", "sync")

    def __init__(self):
        self.ops = {e: [] for e in self.ENG}

    def op(self, e, meth, *a, _inc=None, _amt=1, **k):
        def f(eng, meth=meth, a=a, k=k, _inc=_inc, _amt=_amt):
            ins = getattr(eng, meth)(*a, **k)
            if _inc is not None:
                ins.then_inc(_inc.h, _amt)
            return ins
        self.ops[e].append(f)
        if _inc is not None:
            _inc.n += _amt

    def wait(self, e, ctr, n=None):
        v = ctr.n if n is None else n
        if v <= 0:
            return
        self.ops[e].append(lambda eng, s=ctr.h, v=v: eng.wait_ge(s, v))

    def emit(self, block):
        def run(stream):
            def body(eng):
                for f in stream:
                    f(eng)
            return body
        block.tensor(run(self.ops["pe"]))
        block.vector(run(self.ops["dve"]))
        block.scalar(run(self.ops["act"]))
        block.gpsimd(run(self.ops["pool"]))
        block.sync(run(self.ops["sync"]))


class DmaQ:

    def __init__(self, P, qname, sems):
        self.P, self.q, self.sems = P, qname, sems
        self.uses = [0] * len(sems)
        self.i = 0

    def start(self, out, in_, **kw):
        r = self.i % len(self.sems)
        ctr = self.sems[r]
        if self.uses[r]:
            self.P.wait(self.q, ctr, 16 * self.uses[r])
        self.P.op(self.q, "dma_start", out, in_, _inc=ctr, _amt=16, **kw)
        self.uses[r] += 1
        self.i += 1
        return (ctr, 16 * self.uses[r])

    def wait(self, e, tok):
        self.P.wait(e, tok[0], tok[1])

    def drain(self, e):
        for r, ctr in enumerate(self.sems):
            if self.uses[r]:
                self.P.wait(e, ctr, 16 * self.uses[r])


def build_nc(debug_keys=(), phase="H", esteps=NCORE - 1):
    nc = bass.Bass("TRN2", target_bir_lowering=False)
    dbgs = {}

    def din(name, shape, dtype=F32):
        return nc.dram_tensor(name, shape, dtype, kind="ExternalInput").ap()

    x_in = din("x", [TC, D])
    xh_in = din("xh", [3, D])
    wq_in = din("wq", [D, D])
    wk_in = din("wk", [D, D])
    wv_in = din("wv", [D, D])
    wb_in = din("wb", [D, H])
    convw_in = din("convw", [128, 3, DC, KCONV])
    normw_in = din("normw", [1, DK])
    wo_in = din("wo", [D, D])
    gatewT_in = din("gatewT", [D, NE])
    eupT_in = din("eupT", [NE, D, 2 * IE])
    edownT_in = din("edownT", [NE, IE, D])
    supT_in = din("supT", [D, 2 * SI])
    sdownT_in = din("sdownT", [SI, D])
    ident_in = din("ident", [128, 128])
    masks_in = din("masks", [128, 3, 128])
    i8_in = din("i8", [128, H * 128])
    headsel_in = din("headsel", [128, DC, H])
    onehot8_in = din("onehot8", [H, H * 128])
    onehotE_in = din("onehotE", [NE, NE * 128])
    selcol_in = din("selcol", [128, NCORE])
    out_hbm = nc.dram_tensor("out", [TC, D], F32, kind="ExternalOutput").ap()

    SPW = 896
    spill = nc.dram_tensor("spill", [NCH * H, 128, SPW], BF16, kind="Internal").ap()
    ag_in = nc.dram_tensor("ag_in", [H * DK, 2 * DK], BF16, kind="Internal").ap()
    ag_out = nc.dram_tensor("ag_out", [NCORE * H * DK, 2 * DK], BF16, kind="Internal").ap()

    P = Prog()
    from contextlib import ExitStack
    ctx = ExitStack()

    def sbuf(name, shape, dtype):
        return ctx.enter_context(nc.sbuf_tensor(name, shape, dtype))

    def sem(name):
        return Ctr(ctx.enter_context(nc.semaphore(name)), name)

    PE = sem("c_pe")
    DV = sem("c_dve")
    AC = sem("c_act")
    CC = sem("c_cc")
    QG = DmaQ(P, "pool", [sem(f"c_g{i}") for i in range(4)])
    QS = DmaQ(P, "sync", [sem(f"c_s{i}") for i in range(4)])

    ident_bf = sbuf("ident_bf", [128, 128], BF16)
    ident_f = sbuf("ident_f", [128, 128], F32)
    i8_bf = sbuf("i8_bf", [128, H, 128], BF16)
    masks_f = sbuf("masks_f", [128, 3, 128], F32)
    convw_sb = sbuf("convw_sb", [128, 3, DC, KCONV], F32)
    headsel_bf = sbuf("headsel_bf", [128, DC, H], BF16)
    selcol_f = sbuf("selcol_f", [128, NCORE], F32)
    normw_f = sbuf("normw_f", [1, DK], F32)
    normw_bc = sbuf("normw_bc", [128, DK], F32)
    gatewT_bf = sbuf("gatewT_bf", [128, DC, NE], BF16)
    wb_bf = sbuf("wb_bf", [128, DC, H], BF16)
    ones_f = sbuf("ones_f", [1, 128], F32)
    consts = sbuf("consts", [128, 8], F32)
    yf = sbuf("yf", [128, DC, 1028], BF16)
    beta_fm = sbuf("beta_fm", [8, 1027], F32)
    beta_col = sbuf("beta_col", [128, NCH, H], F32)
    ps = ctx.enter_context(nc.psum_tensor("ps", [128, 8, 512], F32))
    psb = ps.ap().bitcast(BF16)

    CONSTS = {}

    def rconst(val):
        val = float(val)
        if val not in CONSTS:
            i = len(CONSTS)
            assert i < 8
            CONSTS[val] = i
            P.op("dve", "memset", consts[:, i:i + 1], val, _inc=DV)
            nc.const_aps.register(F32, val, consts[:, i:i + 1])
        return val

    for dst, srcap in ((ident_f[:, :], ident_in), (masks_f[:, :, :], masks_in),
                       (convw_sb[:, :, :, :], convw_in), (selcol_f[:, :], selcol_in),
                       (normw_f[:, :], normw_in)):
        QS.start(dst, srcap)
    for dst, srcap in ((ident_bf[:, :], ident_in),
                       (i8_bf[:, :, :], i8_in.rearrange("p (h x) -> p h x", h=H)),
                       (headsel_bf[:, :, :], headsel_in),
                       (gatewT_bf[:, :, :], gatewT_in.rearrange("(c p) e -> p c e", p=128)),
                       (wb_bf[:, :, :], wb_in.rearrange("(c p) h -> p c h", p=128))):
        QG.start(dst, srcap)
    P.op("dve", "memset", ones_f[:, :], 1.0, _inc=DV)
    P.op("dve", "memset", beta_fm[0:8, 0:3], 0.0, _inc=DV)
    P.op("dve", "memset", yf[:, :, 1027:1028], 0.0, _inc=DV)
    rconst(LN_EPS)
    QS.drain("pe")
    QG.drain("pe")
    P.wait("pe", DV)
    P.op("pe", "matmul", ps[:, 0, 0:DK], ones_f[:, :], normw_f[:, :],
         start=True, stop=True, _inc=PE)
    P.wait("dve", PE)
    P.op("dve", "tensor_copy", normw_bc[:, :], ps[:, 0, 0:DK], _inc=DV)
    QS.drain("dve"); QS.drain("act"); QG.drain("dve"); QG.drain("act")
    A_done = dict(pe=PE.n, dve=DV.n, act=AC.n)

    x_rows = x_in.rearrange("(r p) d -> r p d", p=128)
    scope_b1 = ExitStack()
    xrow = scope_b1.enter_context(nc.sbuf_tensor("xrow", [128, 2, D], F32))
    ytm = scope_b1.enter_context(nc.sbuf_tensor("ytm", [128, 2, D], BF16))
    lnst = scope_b1.enter_context(nc.sbuf_tensor("lnst", [128, 8], F32))
    war_x = [0, 0]
    war_ytm = [0, 0]
    tr_evac = [0, 0]
    for r in range(9):
        buf = r % 2
        nrow = 128 if r < 8 else 3
        src = x_rows[r, :, :] if r < 8 else xh_in
        P.wait("sync", DV, war_x[buf])
        tok = QS.start(xrow[0:nrow, buf, :], src)
        QS.wait("act", tok)
        P.wait("act", PE, war_ytm[buf])
        P.op("act", "activation", ytm[0:nrow, buf, :], xrow[0:nrow, buf, :], AF.Identity,
             accum_out=lnst[0:nrow, 0 + buf:1 + buf], _inc=AC)
        P.wait("act", AC)
        P.op("act", "activation", ytm[0:nrow, buf, :], xrow[0:nrow, buf, :], AF.Square,
             accum_out=lnst[0:nrow, 2 + buf:3 + buf], _inc=AC)
        P.wait("dve", AC)
        c0, c2, c4, c6 = (0 + buf, 2 + buf, 4 + buf, 6 + buf)
        P.op("dve", "tensor_scalar", lnst[0:nrow, c0:c0 + 1], lnst[0:nrow, c0:c0 + 1],
             1.0 / D, None, ALU.mult, _inc=DV)
        P.op("dve", "tensor_scalar", lnst[0:nrow, c2:c2 + 1], lnst[0:nrow, c2:c2 + 1],
             1.0 / D, None, ALU.mult, _inc=DV)
        P.wait("dve", DV)
        P.op("dve", "tensor_mul", lnst[0:nrow, c4:c4 + 1], lnst[0:nrow, c0:c0 + 1],
             lnst[0:nrow, c0:c0 + 1], _inc=DV)
        P.wait("dve", DV)
        P.op("dve", "tensor_sub", lnst[0:nrow, c2:c2 + 1], lnst[0:nrow, c2:c2 + 1],
             lnst[0:nrow, c4:c4 + 1], _inc=DV)
        P.wait("act", DV)
        P.op("act", "activation", lnst[0:nrow, c4:c4 + 1], lnst[0:nrow, c2:c2 + 1],
             AF.Sqrt, bias=rconst(LN_EPS), _inc=AC)
        P.wait("dve", AC)
        P.op("dve", "reciprocal", lnst[0:nrow, c6:c6 + 1], lnst[0:nrow, c4:c4 + 1], _inc=DV)
        P.wait("dve", DV)
        P.op("dve", "tensor_scalar", ytm[0:nrow, buf, :], xrow[0:nrow, buf, :],
             lnst[0:nrow, c0:c0 + 1], lnst[0:nrow, c6:c6 + 1],
             ALU.subtract, ALU.mult, _inc=DV)
        war_x[buf] = DV.n
        col0 = 3 + r * 128 if r < 8 else 0
        P.wait("pe", DV)
        for c in range(DC):
            bank = c % 2
            P.wait("pe", AC, tr_evac[bank])
            idb = ident_bf[:, :] if nrow == 128 else ident_bf[0:3, 0:3]
            P.op("pe", "transpose", psb[:, 6 + bank, 0:nrow],
                 ytm[0:nrow, buf, c * 128:(c + 1) * 128], idb, _inc=PE)
            P.wait("act", PE)
            P.op("act", "copy", yf[:, c, col0:col0 + nrow], psb[:, 6 + bank, 0:nrow],
                 _inc=AC)
            tr_evac[bank] = AC.n
        war_ytm[buf] = PE.n
    scope_b1.close()
    yf_done = AC.n

    scope_qkv = ExitStack()

    def sbq(name, shape, dtype):
        return scope_qkv.enter_context(nc.sbuf_tensor(name, shape, dtype))

    q_bf = sbq("q_bf", [128, DC, TC], BF16)
    k_bf = sbq("k_bf", [128, DC, TC], BF16)
    v_bf = sbq("v_bf", [128, DC, TC], BF16)

    scope_b2 = ExitStack()
    wqt = scope_b2.enter_context(nc.sbuf_tensor("wqt", [128, 2, DC, 512], BF16))
    qext = scope_b2.enter_context(nc.sbuf_tensor("qext", [128, 4, 348], F32))
    cacc = scope_b2.enter_context(nc.sbuf_tensor("cacc", [128, 4, 344], F32))
    sqb = scope_b2.enter_context(nc.sbuf_tensor("sqb", [128, 4, 344], BF16))
    tails = scope_b2.enter_context(nc.sbuf_tensor("tails", [128, 4, 3], F32))
    rss = scope_b2.enter_context(nc.sbuf_tensor("rss", [8, 4, 344], F32))
    onehot8_f = scope_b2.enter_context(nc.sbuf_tensor("onehot8_f", [H, H * 128], F32))

    rconst(256.0 * L2_EPS)
    rconst(1.0 * L2_EPS)
    projs = [("q", wq_in, q_bf, 0, 256.0), ("k", wk_in, k_bf, 1, 1.0),
             ("v", wv_in, v_bf, 2, 0.0)]
    war_w = {0: 0, 1: 0}
    pr_evac = [0, 0]
    rb_evac = [0, 0]
    war_qext = 0
    oh8_tok = QS.start(onehot8_f[:, :], onehot8_in)
    QS.wait("pe", oh8_tok)
    for pj, (pname, w_in, obuf, pidx, l2s) in enumerate(projs):
        w_re = w_in.rearrange("(c p) o -> p c o", p=128)
        for qtr in range(4):
            gq = pj * 4 + qtr
            wb_i = gq % 2
            P.wait("pool", PE, war_w[wb_i])
            tok = QG.start(wqt[:, wb_i, :, :], w_re[:, :, qtr * 512:(qtr + 1) * 512])
            QG.wait("pe", tok)
            if gq == 0:
                P.wait("pe", AC, yf_done)
            for ti, (toff, tn) in enumerate(PTILES):
                lo = 3 if ti == 0 else 0
                tok0 = toff - 3 + lo
                nv = tn - lo
                for oci in range(4):
                    bank = oci % 2
                    P.wait("pe", DV, war_qext)
                    P.wait("act", DV, war_qext)
                    P.wait("pe", AC, pr_evac[bank])
                    for dc in range(DC):
                        P.op("pe", "matmul", ps[:, bank, 0:tn],
                             wqt[:, wb_i, dc, oci * 128:(oci + 1) * 128],
                             yf[:, dc, toff:toff + tn],
                             start=(dc == 0), stop=(dc == DC - 1),
                             _inc=(PE if dc == DC - 1 else None))
                    P.wait("act", PE)
                    P.op("act", "copy", qext[:, oci, 3:3 + tn], ps[:, bank, 0:tn],
                         _inc=AC)
                    pr_evac[bank] = AC.n
                P.wait("dve", AC)
                if ti > 0:
                    P.op("dve", "tensor_copy", qext[:, :, 0:3], tails[:, :, :], _inc=DV)
                else:
                    P.op("dve", "memset", qext[:, :, 0:3], 0.0, _inc=DV)
                P.wait("dve", DV)
                cw = convw_sb[:, pidx, qtr * 4:(qtr + 1) * 4, :]
                P.op("dve", "tensor_tensor", cacc[:, :, 0:tn], qext[:, :, 0:tn],
                     cw[:, :, 0:1].broadcast_to((128, 4, tn)), ALU.mult, _inc=DV)
                for i in range(1, KCONV):
                    P.wait("dve", DV)
                    for ci in range(4):
                        P.op("dve", "scalar_tensor_tensor", cacc[:, ci, 0:tn],
                             qext[:, ci, i:i + tn], cw[:, ci, i:i + 1],
                             cacc[:, ci, 0:tn], ALU.mult, ALU.add,
                             _inc=(DV if ci == 3 else None))
                P.wait("dve", DV)
                P.op("dve", "tensor_copy", tails[:, :, :], qext[:, :, tn:tn + 3], _inc=DV)
                war_qext = DV.n
                P.wait("act", DV)
                if l2s == 0.0:
                    P.op("act", "activation", obuf[:, qtr * 4:(qtr + 1) * 4, tok0:tok0 + nv],
                         cacc[:, :, lo:tn], AF.Silu, _inc=AC)
                    war_qext = DV.n
                else:
                    sil = qext[:, :, 4:348]
                    P.op("act", "activation", sil[:, :, lo:tn], cacc[:, :, lo:tn], AF.Silu,
                         _inc=AC)
                    P.wait("act", AC)
                    P.op("act", "activation", sqb[:, :, lo:tn], sil[:, :, lo:tn], AF.Square,
                         _inc=AC)
                    P.wait("pe", AC)
                    for ci in range(4):
                        dc = qtr * 4 + ci
                        P.op("pe", "matmul", ps[0:8, 4, lo:tn], headsel_bf[:, dc, :],
                             sqb[:, ci, lo:tn], start=(ci == 0), stop=(ci == 3),
                             _inc=(PE if ci == 3 else None))
                    P.wait("act", PE)
                    P.op("act", "activation", rss[0:8, 0, lo:tn], ps[0:8, 4, lo:tn],
                         AF.Sqrt, bias=rconst(l2s * L2_EPS), scale=rconst(l2s), _inc=AC)
                    P.wait("dve", AC)
                    P.op("dve", "reciprocal", rss[0:8, 1, lo:tn], rss[0:8, 0, lo:tn],
                         _inc=DV)
                    P.wait("pe", DV)
                    for hh in range(2):
                        h = qtr * 2 + hh
                        bank = 5 + hh
                        P.wait("pe", DV, rb_evac[hh])
                        P.op("pe", "matmul", ps[:, bank, lo:tn],
                             onehot8_f[:, h * 128:(h + 1) * 128], rss[0:8, 1, lo:tn],
                             start=True, stop=True, _inc=PE)
                        P.wait("dve", PE)
                        for cc_ in range(2):
                            dc = qtr * 4 + hh * 2 + cc_
                            P.op("dve", "tensor_mul", obuf[:, dc, tok0:tok0 + nv],
                                 sil[:, hh * 2 + cc_, lo:tn], ps[:, bank, lo:tn],
                                 _inc=(DV if cc_ == 1 else None))
                        rb_evac[hh] = DV.n
                    war_qext = DV.n
            war_w[wb_i] = PE.n
            if gq == 0:
                pass
    for ti, (toff, tn) in enumerate(PTILES):
        lo = 3 if ti == 0 else 0
        P.wait("pe", AC)
        for dc in range(DC):
            P.op("pe", "matmul", ps[0:8, 4, 0:tn], wb_bf[:, dc, :],
                 yf[:, dc, toff:toff + tn], start=(dc == 0), stop=(dc == DC - 1),
                 _inc=(PE if dc == DC - 1 else None))
        P.wait("act", PE)
        P.op("act", "activation", beta_fm[0:8, toff + lo:toff + tn], ps[0:8, 4, lo:tn],
             AF.Sigmoid, _inc=AC)
    scope_b2.close()
    B_done = dict(pe=PE.n, dve=DV.n, act=AC.n)

    def dump(key, src_sb, shape, deps=()):
        if key not in debug_keys:
            return
        t = nc.dram_tensor("dbg_" + key, shape, F32, kind="ExternalOutput").ap()
        dbgs[key] = t
        P.wait("pool", DV)
        P.wait("pool", AC)
        P.wait("pool", PE)
        tok = QG.start(t, src_sb)
        for e in ("pe", "dve", "act", "sync"):
            QG.wait(e, tok)

    dump("q", q_bf[:, :, :], [128, DC, TC])
    dump("k", k_bf[:, :, :], [128, DC, TC])
    dump("v", v_bf[:, :, :], [128, DC, TC])
    dump("beta", beta_fm[0:8, :], [8, 1027])
    dump("yfd", yf[:, :, :], [128, DC, 1028])

    def finish_early():
        P.wait("sync", AC)
        P.wait("sync", DV)
        P.wait("sync", PE)
        zer2 = sbuf("zer2", [128, 16], F32)
        P.op("dve", "memset", zer2[:, :], 0.0, _inc=DV)
        P.wait("sync", DV)
        QS.start(out_hbm[0:128, 0:16], zer2[:, :])
        QS.drain("sync")
        QG.drain("sync")
        with nc.Block() as block:
            P.emit(block)
        ctx.close()
        return nc, dbgs

    if phase == "B":
        return finish_early()

    scope_c = ExitStack()

    def sbc(name, shape, dtype):
        return scope_c.enter_context(nc.sbuf_tensor(name, shape, dtype))

    PD_f = sbc("PD_f", [128, H, 2, 512], F32)
    PD_bf = sbc("PD_bf", [128, 2, 512], BF16)
    N8 = sbc("N8", [128, H, 128], BF16)
    M8 = sbc("M8", [128, H, 128], BF16)
    X8 = sbc("X8", [128, H, 128], BF16)
    Y8 = sbc("Y8", [128, H, 128], BF16)
    G8 = sbc("G8", [128, H, 128], BF16)
    H8 = sbc("H8", [128, H, 128], BF16)
    TT8 = sbc("TT8", [128, H, 128], BF16)
    Gm8 = sbc("Gm8", [128, H, 128], BF16)
    KT = sbc("KT", [128, 2, 128], BF16)
    VT = sbc("VT", [128, 256], BF16)
    XE = sbc("XE", [128, 512], BF16)
    UE = sbc("UE", [128, 512], BF16)
    nbU = sbc("nbU", [128, 128], F32)
    brow_s = sbc("brow_s", [1, 128], F32)
    stage = sbc("stage", [128, 2, SPW], BF16)

    P.wait("dve", AC, B_done["act"])
    P.op("dve", "memset", PD_f[:, :, :, :], 0.0, _inc=DV)
    P.wait("dve", DV)
    for h in range(H):
        for j in range(2):
            P.op("dve", "tensor_copy", PD_f[:, h, j, j * 128:(j + 1) * 128],
                 ident_f[:, :], _inc=(DV if (h, j) == (H - 1, 1) else None))
    pd_init_dv = DV.n

    stg_tok = [None, None]
    for c in range(NCH):
        tc0 = c * CH
        yc0 = tc0 + 3
        P.wait("pe", AC)
        P.wait("pe", DV)
        P.op("pe", "transpose", ps[0:128, 2, 0:8], beta_fm[0:8, yc0:yc0 + 128],
             ident_f[0:8, 0:8], _inc=PE)
        P.wait("dve", PE)
        P.op("dve", "tensor_copy", beta_col[:, c, :], ps[:, 2, 0:8], _inc=DV)
        bcol_dv = DV.n
        for h in range(H):
            for j in range(2):
                P.op("pe", "matmul", ps[:, h // 4, (h % 4) * 128:(h % 4 + 1) * 128],
                     k_bf[:, 2 * h + j, tc0:tc0 + CH], k_bf[:, 2 * h + j, tc0:tc0 + CH],
                     start=(j == 0), stop=(j == 1), _inc=(PE if j == 1 else None))
        a_pe = PE.n
        for h in range(H):
            P.wait("pe", DV, bcol_dv)
            P.wait("pe", DV)
            P.op("pe", "transpose", ps[0:1, 2, 128:256], beta_col[:, c, h:h + 1],
                 ident_f[:, :], _inc=PE)
            P.wait("dve", PE)
            P.op("dve", "tensor_copy", brow_s[:, :], ps[0:1, 2, 128:256], _inc=DV)
            P.wait("pe", DV)
            P.op("pe", "matmul", ps[:, 3, 0:128], ones_f[:, :], brow_s[:, :],
                 start=True, stop=True, _inc=PE)
            P.wait("dve", PE, a_pe)
            P.wait("dve", PE)
            P.op("dve", "tensor_mul", nbU[:, :], ps[:, 3, 0:128], masks_f[:, 1, :],
                 _inc=DV)
            P.wait("dve", DV)
            P.op("dve", "tensor_mul", N8[:, h, :],
                 ps[:, h // 4, (h % 4) * 128:(h % 4 + 1) * 128], nbU[:, :], _inc=DV)
            P.wait("pe", DV)
            P.wait("pe", AC, m_evac_war[h % 2] if (c, h) >= (0, 2) else 0)
            P.op("pe", "transpose", psb[:, 6 + h % 2, 0:128],
                 N8[:, h, :], ident_bf[:, :], _inc=PE)
            if (c, h) == (0, 0):
                m_evac_war = [0, 0]
            P.wait("act", PE)
            P.op("act", "copy", M8[:, h, :], psb[:, 6 + h % 2, 0:128],
                 _inc=AC)
            m_evac_war[h % 2] = AC.n
        P.wait("dve", AC)
        P.op("dve", "tensor_sub", X8[:, :, :], i8_bf[:, :, :], N8[:, :, :], _inc=DV)
        P.op("dve", "tensor_sub", Y8[:, :, :], i8_bf[:, :, :], M8[:, :, :], _inc=DV)
        xy_dv = DV.n
        gh_ac = AC.n
        for lv in range(1, 7):
            gsrc, hsrc = (N8, M8) if lv == 1 else (G8, H8)
            P.wait("pe", AC, gh_ac)
            P.wait("pe", DV, xy_dv)
            for h in range(H):
                P.op("pe", "matmul", ps[:, 4 + h // 4, (h % 4) * 128:(h % 4 + 1) * 128],
                     hsrc[:, h, :], gsrc[:, h, :], start=True, stop=True)
                P.op("pe", "matmul", ps[:, 6 + h // 4, (h % 4) * 128:(h % 4 + 1) * 128],
                     gsrc[:, h, :], hsrc[:, h, :], start=True, stop=True,
                     _inc=(PE if h == H - 1 else None))
            P.wait("act", PE)
            P.op("act", "copy", G8[:, :, :], ps[:, 4:6, :].rearrange("p b x -> p (b x)")
                 .rearrange("p (a y) -> p a y", a=H), _inc=AC)
            P.op("act", "copy", H8[:, :, :], ps[:, 6:8, :].rearrange("p b x -> p (b x)")
                 .rearrange("p (a y) -> p a y", a=H), _inc=AC)
            gh_ac = AC.n
            P.wait("pe", AC, gh_ac)
            for h in range(H):
                P.op("pe", "matmul", ps[:, h // 4, (h % 4) * 128:(h % 4 + 1) * 128],
                     Y8[:, h, :], G8[:, h, :], start=True, stop=True)
                P.op("pe", "matmul", ps[:, 2 + h // 4, (h % 4) * 128:(h % 4 + 1) * 128],
                     G8[:, h, :], Y8[:, h, :], start=True, stop=True,
                     _inc=(PE if h == H - 1 else None))
            P.wait("dve", PE)
            P.op("dve", "tensor_add", X8[:, :, :],
                 ps[:, 0:2, :].rearrange("p b x -> p (b x)").rearrange("p (a y) -> p a y", a=H),
                 X8[:, :, :], _inc=DV)
            P.op("dve", "tensor_add", Y8[:, :, :],
                 ps[:, 2:4, :].rearrange("p b x -> p (b x)").rearrange("p (a y) -> p a y", a=H),
                 Y8[:, :, :], _inc=DV)
            xy_dv = DV.n
        P.wait("dve", DV, xy_dv)
        for h in range(H):
            P.op("dve", "tensor_scalar", TT8[:, h, :], X8[:, h, :],
                 beta_col[:, c, h:h + 1], None, ALU.mult,
                 _inc=(DV if h == H - 1 else None))
        tt_dv = DV.n
        for h in range(H):
            P.wait("pe", DV)
            for j in range(2):
                P.op("pe", "matmul", ps[:, 2 + h % 2, 0:128],
                     k_bf[:, 2 * h + j, tc0:tc0 + CH], q_bf[:, 2 * h + j, tc0:tc0 + CH],
                     start=(j == 0), stop=(j == 1), _inc=(PE if j == 1 else None))
            P.wait("dve", PE)
            P.op("dve", "tensor_mul", Gm8[:, h, :], ps[:, 2 + h % 2, 0:128],
                 masks_f[:, 2, :], _inc=DV)
        gm_dv = DV.n
        for h in range(H):
            sb = (c * H + h) % 2
            P.wait("pe", AC)
            for j in range(2):
                P.op("pe", "transpose", psb[:, 7, j * 128:(j + 1) * 128],
                     k_bf[:, 2 * h + j, tc0:tc0 + CH], ident_bf[:, :], _inc=PE)
                P.op("pe", "transpose", psb[:, 7, (2 + j) * 128:(3 + j) * 128],
                     v_bf[:, 2 * h + j, tc0:tc0 + CH], ident_bf[:, :], _inc=PE)
            P.wait("act", PE)
            if (c, h) != (0, 0):
                P.wait("act", PE, ktprot_pe)
                P.wait("act", DV, ktprot_dv)
            P.op("act", "copy", KT[:, 0, :], psb[:, 7, 0:128])
            P.op("act", "copy", KT[:, 1, :], psb[:, 7, 128:256])
            P.op("act", "copy", VT[:, 0:128], psb[:, 7, 256:384])
            P.op("act", "copy", VT[:, 128:256], psb[:, 7, 384:512], _inc=AC)
            ktvt_ac = AC.n
            P.wait("dve", DV, pd_init_dv)
            P.wait("dve", DV, pdupd_dv if (c, h) != (0, 0) else 0)
            P.op("dve", "tensor_copy", PD_bf[:, :, :], PD_f[:, h, :, :], _inc=DV)
            pdbf_dv = DV.n
            P.wait("pe", DV, pdbf_dv)
            P.wait("pe", DV, xe_war if (c, h) != (0, 0) else 0)
            for j in range(2):
                P.op("pe", "matmul", ps[:, 0, :], k_bf[:, 2 * h + j, tc0:tc0 + CH],
                     PD_bf[:, j, :], start=(j == 0), stop=(j == 1),
                     _inc=(PE if j == 1 else None))
            P.wait("dve", PE)
            P.wait("dve", AC, ktvt_ac)
            P.op("dve", "tensor_scalar", XE[:, 0:256], ps[:, 0, 0:256], -1.0, None,
                 ALU.mult, _inc=DV)
            P.op("dve", "tensor_sub", XE[:, 256:512], VT[:, :], ps[:, 0, 256:512],
                 _inc=DV)
            xe_dv = DV.n
            ktprot_dv = DV.n
            P.wait("pe", DV, xe_dv)
            P.op("pe", "matmul", ps[:, 1, :], TT8[:, h, :], XE[:, :],
                 start=True, stop=True, _inc=PE)
            P.wait("dve", PE)
            P.op("dve", "tensor_copy", UE[:, :], ps[:, 1, :], _inc=DV)
            ue_dv = DV.n
            xe_war = DV.n
            P.wait("pe", DV, ue_dv)
            if stg_tok[sb] is not None:
                QS.wait("act", stg_tok[sb])
                QS.wait("dve", stg_tok[sb])
            for j in range(2):
                P.op("pe", "transpose", psb[:, 7, (4 + j) * 128:(5 + j) * 128],
                     UE[:, j * 128:(j + 1) * 128], ident_bf[:, :],
                     _inc=(PE if j == 1 else None))
            P.wait("act", PE)
            P.op("act", "copy", stage[:, sb, 256:384], psb[:, 7, 512:640])
            P.op("act", "copy", stage[:, sb, 384:512], psb[:, 7, 640:768], _inc=AC)
            P.wait("pe", DV, gm_dv)
            for aj in range(2):
                for dj in range(2):
                    P.op("pe", "matmul", ps[:, 2, aj * 128:(aj + 1) * 128],
                         PD_bf[:, dj, aj * 128:(aj + 1) * 128],
                         q_bf[:, 2 * h + dj, tc0:tc0 + CH],
                         start=(dj == 0), stop=(dj == 1),
                         _inc=(PE if (aj, dj) == (1, 1) else None))
            P.wait("act", PE)
            P.op("act", "copy", stage[:, sb, 512:768], ps[:, 2, 0:256], _inc=AC)
            P.wait("pe", DV, ue_dv)
            for j in range(2):
                P.op("pe", "matmul", ps[:, 3, 0:256], q_bf[:, 2 * h + j, tc0:tc0 + CH],
                     PD_bf[:, j, 256:512], start=(j == 0), stop=False)
            P.op("pe", "matmul", ps[:, 3, 0:256], Gm8[:, h, :], UE[:, 256:512],
                 start=False, stop=True, _inc=PE)
            P.wait("act", PE)
            P.op("act", "copy", stage[:, sb, 0:256], ps[:, 3, 0:256], _inc=AC)
            P.wait("dve", DV, gm_dv)
            P.op("dve", "tensor_copy", stage[:, sb, 768:896], Gm8[:, h, :], _inc=DV)
            for aj in range(2):
                P.op("pe", "matmul", ps[:, 4 + aj, :], KT[:, aj, :], UE[:, :],
                     start=True, stop=True, _inc=(PE if aj == 1 else None))
            P.wait("dve", PE)
            for aj in range(2):
                P.op("dve", "tensor_add", PD_f[:, h, aj, :], ps[:, 4 + aj, :],
                     PD_f[:, h, aj, :], _inc=(DV if aj == 1 else None))
            pdupd_dv = DV.n
            ktprot_pe = PE.n
            P.wait("sync", AC)
            P.wait("sync", DV)
            stg_tok[sb] = QS.start(spill[c * H + h, :, :], stage[:, sb, :])
    agst = stage[:, :, 0:512]
    ag_toks = []
    for h in range(H):
        P.wait("dve", DV, pdupd_dv)
        for tk in stg_tok:
            if tk is not None:
                QS.wait("dve", tk)
                QS.wait("act", tk)
        stg_tok = [None, None]
        if ag_toks:
            QS.wait("dve", ag_toks[-1])
            QS.wait("act", ag_toks[-1])
        P.op("dve", "tensor_copy", PD_bf[:, :, :], PD_f[:, h, :, :], _inc=DV)
        P.op("dve", "tensor_copy", agst[:, 0, 256:512], PD_f[:, h, 0, 256:512])
        P.op("dve", "tensor_copy", agst[:, 1, 256:512], PD_f[:, h, 1, 256:512], _inc=DV)
        P.wait("pe", DV)
        P.wait("pe", AC)
        for dj in range(2):
            for aj in range(2):
                P.op("pe", "transpose", psb[:, 7, (dj * 2 + aj) * 128:(dj * 2 + aj + 1) * 128],
                     PD_bf[:, aj, dj * 128:(dj + 1) * 128], ident_bf[:, :],
                     _inc=(PE if (dj, aj) == (1, 1) else None))
        P.wait("act", PE)
        for dj in range(2):
            for aj in range(2):
                P.op("act", "copy", agst[:, dj, aj * 128:(aj + 1) * 128],
                     psb[:, 7, (dj * 2 + aj) * 128:(dj * 2 + aj + 1) * 128],
                     _inc=(AC if (dj, aj) == (1, 1) else None))
        P.wait("sync", AC)
        P.wait("sync", DV)
        ag_toks.append(QS.start(
            ag_in[h * DK:(h + 1) * DK, :].rearrange("(j p) x -> p j x", p=128),
            agst[:, :, :]))
    scope_c.close()
    scope_qkv.close()
    C_done = dict(pe=PE.n, dve=DV.n, act=AC.n)
    if phase == "C":
        QS.drain("sync")
        return finish_early()

    for t in ag_toks:
        QS.wait("pool", t)
    P.op("pool", "collective_compute", "AllGather", ALU.bypass,
         replica_groups=[list(range(NCORE))],
         ins=[ag_in.opt()], outs=[ag_out.opt()], _inc=CC)

    out_acc = sbuf("out_acc", [128, 8, D], F32)
    scope_ef = ExitStack()
    wo_sb = scope_ef.enter_context(nc.sbuf_tensor("wo_sb", [128, DC, D], BF16))
    smine_bf = scope_ef.enter_context(nc.sbuf_tensor("smine_bf", [128, H, 2, DK], BF16))

    if phase == "D":
        P.wait("sync", CC, 1)
        return finish_early()

    P.wait("pool", CC, 1)
    wo_tok = QG.start(wo_sb[:, :, :], wo_in.rearrange("(c p) o -> p c o", p=128))

    scope_e = ExitStack()
    pdstep = scope_e.enter_context(nc.sbuf_tensor("pdstep", [128, H, 2, 512], BF16))
    Sall = scope_e.enter_context(nc.sbuf_tensor("Sall", [128, H, 2, DK], BF16))

    P.wait("dve", CC, 1)
    P.op("dve", "memset", Sall[:, :, :, :], 0.0, _inc=DV)
    P.op("dve", "memset", smine_bf[:, :, :, :], 0.0, _inc=DV)
    sall_dv = DV.n
    step_tok = None
    for i in range(esteps):
        P.wait("pool", CC, 1)
        if step_tok is not None:
            P.wait("pool", PE, pd_consumed)
        step_tok = QG.start(
            pdstep[:, :, :, :],
            ag_out[i * H * DK:(i + 1) * H * DK, :].rearrange("(h j p) x -> p h j x", p=128, j=2))
        QG.wait("pe", step_tok)
        P.wait("pe", DV, sall_dv)
        for h in range(H):
            P.wait("pe", DV, schain_war if (i, h) != (0, 0) else 0)
            P.wait("pe", AC, schain_war_ac if (i, h) != (0, 0) else 0)
            for aj in range(2):
                for dj in range(2):
                    P.op("pe", "matmul", ps[:, aj, 0:DK],
                         pdstep[:, h, dj, aj * 128:(aj + 1) * 128], Sall[:, h, dj, :],
                         start=(dj == 0), stop=False)
                P.op("pe", "matmul", ps[:, aj, 0:DK], ident_bf[:, :],
                     pdstep[:, h, aj, 256:512], start=False, stop=True,
                     _inc=(PE if aj == 1 else None))
            P.wait("dve", PE)
            for aj in range(2):
                P.op("dve", "scalar_tensor_tensor", smine_bf[:, h, aj, :],
                     ps[:, aj, 0:DK], selcol_f[:, i + 1:i + 2], smine_bf[:, h, aj, :],
                     ALU.mult, ALU.add, _inc=DV)
                P.wait("dve", DV)
                P.op("dve", "tensor_copy", Sall[:, h, aj, :], ps[:, aj, 0:DK],
                     _inc=(DV if aj == 1 else None))
            schain_war = DV.n
            schain_war_ac = AC.n
        pd_consumed = PE.n
    P.op("dve", "engine_nop", _inc=DV)
    smine_dv = DV.n
    dump("smine", smine_bf[:, :, :, :], [128, H, 2, DK])
    scope_e.close()
    if phase == "E":
        return finish_early()

    scope_f = ExitStack()

    def sbf(name, shape, dtype):
        return scope_f.enter_context(nc.sbuf_tensor(name, shape, dtype))

    spin = sbf("spin", [128, 2, SPW], BF16)
    ofm = sbf("ofm", [128, DC, CH], BF16)
    ucb = sbf("ucb", [128, DK], BF16)
    orow = sbf("orow", [128, DK], F32)
    sq2 = sbf("sq2", [128, DK], F32)
    onst = sbf("onst", [128, 8], F32)
    xr2 = sbf("xr2", [128, 1, D], F32)
    y2tm = sbf("y2tm", [128, D], BF16)

    QS.drain("sync")
    rconst(RMS_EPS)
    P.wait("sync", DV, smine_dv)
    spin_tok = [None, None]
    x_tok = [None, None]
    ofm_war = 0
    uc_war = 0
    or_war = 0
    y2_war = 0
    for c in range(NCH):
        xb = 0
        if x_tok[xb] is not None:
            P.wait("sync", DV, x2_done)
        x_tok[xb] = QS.start(xr2[:, xb, :], x_rows[c, :, :])
        for h in range(H):
            sb = (c * H + h) % 2
            if spin_tok[sb] is not None:
                P.wait("sync", PE, sp_pe_done)
                P.wait("sync", DV, sp_dv_done)
            spin_tok[sb] = QS.start(spin[:, sb, :], spill[c * H + h, :, :])
            QS.wait("pe", spin_tok[sb])
            P.wait("pe", DV, smine_dv)
            P.wait("pe", DV, uc_war)
            for aj in range(2):
                P.op("pe", "matmul", ps[:, 0, 0:DK],
                     spin[:, sb, 256 + aj * 128:256 + (aj + 1) * 128],
                     smine_bf[:, h, aj, :], start=(aj == 0), stop=(aj == 1),
                     _inc=(PE if aj == 1 else None))
            P.wait("dve", PE)
            P.op("dve", "tensor_copy", ucb[:, :], ps[:, 0, 0:DK], _inc=DV)
            uc_dv = DV.n
            P.wait("pe", DV, uc_dv)
            P.wait("pe", DV, or_war)
            for aj in range(2):
                P.op("pe", "matmul", ps[:, 1, 0:DK],
                     spin[:, sb, 512 + aj * 128:512 + (aj + 1) * 128],
                     smine_bf[:, h, aj, :], start=(aj == 0), stop=False)
            P.op("pe", "matmul", ps[:, 1, 0:DK], spin[:, sb, 768:896], ucb[:, :],
                 start=False, stop=True, _inc=PE)
            uc_war = uc_dv
            sp_pe_done = PE.n
            P.wait("dve", PE)
            P.op("dve", "tensor_add", orow[:, :], ps[:, 1, 0:DK], spin[:, sb, 0:256],
                 _inc=DV)
            sp_dv_done = DV.n
            P.wait("act", DV)
            P.op("act", "activation", sq2[:, :], orow[:, :], AF.Square,
                 accum_out=onst[:, 0:1], _inc=AC)
            P.wait("dve", AC)
            P.op("dve", "tensor_scalar", onst[:, 1:2], onst[:, 0:1], 1.0 / DK, None,
                 ALU.mult, _inc=DV)
            P.wait("act", DV)
            P.op("act", "activation", onst[:, 2:3], onst[:, 1:2], AF.Sqrt,
                 bias=rconst(RMS_EPS), _inc=AC)
            P.wait("dve", AC)
            P.op("dve", "reciprocal", onst[:, 3:4], onst[:, 2:3], _inc=DV)
            P.wait("dve", DV)
            P.op("dve", "tensor_scalar", sq2[:, :], orow[:, :], onst[:, 3:4], None,
                 ALU.mult, _inc=DV)
            or_war = DV.n
            P.wait("dve", DV)
            P.op("dve", "tensor_mul", sq2[:, :], sq2[:, :], normw_bc[:, :], _inc=DV)
            on_dv = DV.n
            P.wait("pe", DV, on_dv)
            P.wait("pe", AC, ofm_war)
            for j in range(2):
                P.op("pe", "transpose", ps[:, 6, j * 128:(j + 1) * 128],
                     sq2[:, j * 128:(j + 1) * 128], ident_f[:, :],
                     _inc=(PE if j == 1 else None))
            P.wait("act", PE)
            for j in range(2):
                P.op("act", "copy", ofm[:, 2 * h + j, :],
                     ps[:, 6, j * 128:(j + 1) * 128],
                     _inc=(AC if j == 1 else None))
            ofm_war = AC.n
        QG.wait("pe", wo_tok)
        P.wait("pe", AC, ofm_war)
        P.wait("pe", DV, wo_war if c > 0 else 0)
        for dq in range(4):
            for kc in range(DC):
                P.op("pe", "matmul", ps[:, 2 + dq, :], ofm[:, kc, :],
                     wo_sb[:, kc, dq * 512:(dq + 1) * 512],
                     start=(kc == 0), stop=(kc == DC - 1),
                     _inc=(PE if kc == DC - 1 else None))
        QS.wait("dve", x_tok[xb])
        P.wait("dve", PE)
        for dq in range(4):
            P.op("dve", "tensor_add", out_acc[:, c, dq * 512:(dq + 1) * 512],
                 ps[:, 2 + dq, :], xr2[:, xb, dq * 512:(dq + 1) * 512],
                 _inc=(DV if dq == 3 else None))
        wo_war = DV.n
        x2_done = DV.n
        P.wait("act", DV, x2_done)
        P.wait("act", PE, y2_war)
        P.op("act", "activation", y2tm[:, :], out_acc[:, c, :], AF.Identity,
             accum_out=onst[:, 4:5], _inc=AC)
        P.wait("act", AC)
        P.op("act", "activation", y2tm[:, :], out_acc[:, c, :], AF.Square,
             accum_out=onst[:, 5:6], _inc=AC)
        P.wait("dve", AC)
        P.op("dve", "tensor_scalar", onst[:, 4:5], onst[:, 4:5], 1.0 / D, None,
             ALU.mult, _inc=DV)
        P.op("dve", "tensor_scalar", onst[:, 5:6], onst[:, 5:6], 1.0 / D, None,
             ALU.mult, _inc=DV)
        P.wait("dve", DV)
        P.op("dve", "tensor_mul", onst[:, 6:7], onst[:, 4:5], onst[:, 4:5], _inc=DV)
        P.wait("dve", DV)
        P.op("dve", "tensor_sub", onst[:, 5:6], onst[:, 5:6], onst[:, 6:7], _inc=DV)
        P.wait("act", DV)
        P.op("act", "activation", onst[:, 6:7], onst[:, 5:6], AF.Sqrt,
             bias=rconst(LN_EPS), _inc=AC)
        P.wait("dve", AC)
        P.op("dve", "reciprocal", onst[:, 7:8], onst[:, 6:7], _inc=DV)
        P.wait("dve", DV)
        P.op("dve", "tensor_scalar", y2tm[:, :], out_acc[:, c, :],
             onst[:, 4:5], onst[:, 7:8], ALU.subtract, ALU.mult, _inc=DV)
        y2_dv = DV.n
        P.wait("pe", DV, y2_dv)
        for cc2 in range(DC):
            bank = cc2 % 2
            P.wait("pe", AC, y2tr_war[bank] if c > 0 or cc2 >= 2 else 0)
            P.op("pe", "transpose", psb[:, 6 + bank, 0:128],
                 y2tm[:, cc2 * 128:(cc2 + 1) * 128], ident_bf[:, :], _inc=PE)
            if c == 0 and cc2 == 0:
                y2tr_war = [0, 0]
            P.wait("act", PE)
            P.op("act", "copy", yf[:, cc2, c * CH:(c + 1) * CH],
                 psb[:, 6 + bank, 0:128], _inc=AC)
            y2tr_war[bank] = AC.n
        y2_war = PE.n
        ofm_war = AC.n
    F_done = dict(pe=PE.n, dve=DV.n, act=AC.n)
    dump("x2", out_acc[:, :, :], [128, 8, D])
    dump("y2f", yf[:, :, 0:1024], [128, DC, 1024])
    scope_f.close()
    scope_ef.close()

    cw_fm = sbuf("cw_fm", [NE, TC], BF16)
    onehotE_bf = sbuf("onehotE_bf", [NE, NE * 128], BF16)
    scope_g = ExitStack()
    gsc = scope_g.enter_context(nc.sbuf_tensor("gsc", [128, 24, 16], F32))
    cwb = scope_g.enter_context(nc.sbuf_tensor("cwb", [128, 16], BF16))
    P.wait("pool", PE, F_done["pe"])
    P.wait("pool", DV, F_done["dve"])
    P.wait("pool", AC, F_done["act"])
    ohe_tok = QG.start(onehotE_bf[:, :], onehotE_in)

    def dvs(meth, *a, **k):
        P.wait("dve", DV)
        P.op("dve", meth, *a, _inc=DV, **k)

    SC = gsc[:, 0, :]
    SCE = gsc[:, 1, :]
    MSC = gsc[:, 2, :]
    EMK = gsc[:, 3, :]
    SMK = gsc[:, 4, :]
    PFa = gsc[:, 5, :]
    PFb = gsc[:, 6, :]
    GS = gsc[:, 7, 0:4]
    GS2 = gsc[:, 7, 4:8]
    EQ1 = gsc[:, 7, 8:12]
    EQ2 = gsc[:, 7, 12:16]
    FO1 = gsc[:, 8, 0:4]
    FO2 = gsc[:, 8, 4:8]
    GMK = gsc[:, 8, 8:12]
    PF4a = gsc[:, 9, 0:4]
    PF4b = gsc[:, 9, 4:8]
    IZ = gsc[:, 9, 8:12]
    EQT = gsc[:, 10, :]
    IZ16 = gsc[:, 11, :]
    STAT = gsc[:, 12, 0:8]

    cwtr_war = 0
    scb_ac = F_done["act"]
    scb_dv = F_done["dve"]
    for tt in range(8):
        P.wait("pe", DV, scb_dv)
        P.wait("pe", AC, scb_ac)
        P.wait("pe", AC, cwtr_war)
        for dc in range(DC):
            P.op("pe", "matmul", ps[0:128, 0, 0:NE], yf[:, dc, tt * 128:(tt + 1) * 128],
                 gatewT_bf[:, dc, :], start=(dc == 0), stop=(dc == DC - 1),
                 _inc=(PE if dc == DC - 1 else None))
        P.wait("dve", PE)
        dvs("reduce_max", STAT[:, 0:1], ps[:, 0, 0:NE], axis=AX.X)
        scb_dv = DV.n
        dvs("tensor_scalar", STAT[:, 1:2], STAT[:, 0:1], -1.0, None, ALU.mult)
        P.wait("act", DV)
        P.op("act", "activation", SCE, ps[:, 0, 0:NE], AF.Exp, bias=STAT[:, 1:2],
             accum_out=STAT[:, 2:3], _inc=AC)
        scb_ac = AC.n
        P.wait("dve", AC)
        dvs("reciprocal", STAT[:, 3:4], STAT[:, 2:3])
        dvs("tensor_scalar", SC, SCE, STAT[:, 3:4], None, ALU.mult)
        dvs("reduce_max", GS, SC.rearrange("p (g j) -> p g j", j=4), axis=AX.X)
        dvs("reduce_max", STAT[:, 4:5], GS, axis=AX.X)
        dvs("tensor_single_scalar", EQ1, GS, STAT[:, 4:5], ALU.is_equal)
        dvs("memset", PFa[:, 0:4], 0.0)
        dvs("tensor_copy", PFa[:, 1:4], EQ1[:, 0:3])
        dvs("tensor_copy", PFb[:, 0:4], PFa[:, 0:4])
        dvs("tensor_add", PFb[:, 2:4], PFa[:, 2:4], PFa[:, 0:2])
        dvs("tensor_single_scalar", IZ, PFb[:, 0:4], 0.5, ALU.is_lt)
        dvs("tensor_mul", FO1, EQ1, IZ)
        dvs("scalar_tensor_tensor", GS2, FO1, -2.0, GS, ALU.mult, ALU.add)
        dvs("reduce_max", STAT[:, 5:6], GS2, axis=AX.X)
        dvs("tensor_single_scalar", EQ2, GS2, STAT[:, 5:6], ALU.is_equal)
        dvs("memset", PFa[:, 0:4], 0.0)
        dvs("tensor_copy", PFa[:, 1:4], EQ2[:, 0:3])
        dvs("tensor_copy", PFb[:, 0:4], PFa[:, 0:4])
        dvs("tensor_add", PFb[:, 2:4], PFa[:, 2:4], PFa[:, 0:2])
        dvs("tensor_single_scalar", IZ, PFb[:, 0:4], 0.5, ALU.is_lt)
        dvs("tensor_mul", FO2, EQ2, IZ)
        dvs("tensor_add", GMK, FO1, FO2)
        dvs("tensor_copy", EMK.rearrange("p (g j) -> p g j", j=4),
            GMK.unsqueeze(2).broadcast_to((128, 4, 4)))
        dvs("tensor_mul", MSC, SC, EMK)
        dvs("memset", SMK, 0.0)
        for t4 in range(4):
            dvs("reduce_max", STAT[:, 6:7], MSC, axis=AX.X)
            dvs("tensor_single_scalar", EQT, MSC, STAT[:, 6:7], ALU.is_equal)
            dvs("memset", PFa[:, 0:16], 0.0)
            dvs("tensor_copy", PFa[:, 1:16], EQT[:, 0:15])
            dvs("tensor_copy", PFb[:, 0:16], PFa[:, 0:16])
            dvs("tensor_add", PFb[:, 2:16], PFa[:, 2:16], PFa[:, 0:14])
            dvs("tensor_copy", PFa[:, 0:16], PFb[:, 0:16])
            dvs("tensor_add", PFa[:, 4:16], PFb[:, 4:16], PFb[:, 0:12])
            dvs("tensor_copy", PFb[:, 0:16], PFa[:, 0:16])
            dvs("tensor_add", PFb[:, 8:16], PFa[:, 8:16], PFa[:, 0:8])
            dvs("tensor_single_scalar", IZ16, PFb[:, 0:16], 0.5, ALU.is_lt)
            dvs("tensor_mul", EQT, EQT, IZ16)
            dvs("tensor_add", SMK, SMK, EQT)
            dvs("scalar_tensor_tensor", MSC, EQT, -2.0, MSC, ALU.mult, ALU.add)
        dvs("tensor_mul", EMK, SC, SMK)
        dvs("reduce_sum", STAT[:, 7:8], EMK, axis=AX.X)
        dvs("tensor_scalar", STAT[:, 7:8], STAT[:, 7:8], 1e-20, None, ALU.add)
        dvs("reciprocal", STAT[:, 6:7], STAT[:, 7:8])
        dvs("tensor_scalar", cwb[:, :], EMK, STAT[:, 6:7], None, ALU.mult)
        P.wait("pe", DV)
        P.wait("pe", AC, cwtr_war if tt > 0 else F_done["act"])
        P.op("pe", "transpose", psb[0:16, 6 + tt % 2, 0:128], cwb[:, :], ident_bf[:, :],
             _inc=PE)
        P.wait("act", PE)
        P.op("act", "copy", cw_fm[0:NE, tt * 128:(tt + 1) * 128],
             psb[0:16, 6 + tt % 2, 0:128], _inc=AC)
        cwtr_war = AC.n
    scope_g.close()
    G_done = dict(pe=PE.n, dve=DV.n, act=AC.n)
    dump("cw", cw_fm[0:NE, :], [NE, TC])

    scope_h = ExitStack()

    def sbh(name, shape, dtype):
        return scope_h.enter_context(nc.sbuf_tensor(name, shape, dtype))

    wst = sbh("wst", [128, 2, 2, DC, 128], BF16)
    cwsb = sbh("cwsb", [128, 2, 512], F32)
    t1 = sbh("t1", [128, 2, 512], F32)

    scope_he = ExitStack()
    dwst = scope_he.enter_context(nc.sbuf_tensor("dwst", [128, ICN, D], BF16))
    a_e = scope_he.enter_context(nc.sbuf_tensor("a_e", [128, ICN, TC], BF16))

    P.wait("pool", PE, G_done["pe"])
    P.wait("pool", DV, G_done["dve"])
    P.wait("pool", AC, G_done["act"])
    QG.wait("pe", ohe_tok)
    P.wait("pe", DV, G_done["dve"])
    P.wait("pe", AC, G_done["act"])
    upps_war = [0, 0, 0, 0]
    t1_war = [0, 0]
    t1dv_war = [0, 0]
    up_pe = [0, 0]
    cw_war = [0, 0]
    ae_war = 0
    dn_war = [0, 0, 0]
    oa_war = G_done["dve"]
    dw_tok = [None] * ICN
    up_tok = [None, None]
    for e in range(NE):
        for i in range(ICN):
            wb_i = i % 2
            if up_tok[wb_i] is not None:
                P.wait("pool", PE, up_pe[wb_i])
            up_tok[wb_i] = QG.start(
                wst[:, wb_i, 0, :, :],
                eupT_in[e, :, i * 128:(i + 1) * 128].rearrange("(c p) o -> p c o", p=128))
            tok_u = QG.start(
                wst[:, wb_i, 1, :, :],
                eupT_in[e, :, IE + i * 128:IE + (i + 1) * 128].rearrange("(c p) o -> p c o", p=128))
            QG.wait("pe", up_tok[wb_i])
            QG.wait("pe", tok_u)
            for tt in range(2):
                if i == 0:
                    P.wait("pe", AC, G_done["act"])
                    P.wait("pe", DV, max(cw_war))
                    P.op("pe", "matmul", ps[:, 4, :], onehotE_bf[:, e * 128:(e + 1) * 128],
                         cw_fm[0:NE, tt * 512:(tt + 1) * 512], start=True, stop=True,
                         _inc=PE)
                    P.wait("dve", PE)
                    P.op("dve", "tensor_copy", cwsb[:, tt, :], ps[:, 4, :], _inc=DV)
                    cw_war[tt] = DV.n
                bg = tt * 2
                bu = tt * 2 + 1
                P.wait("pe", AC, t1_war[tt])
                for dc in range(DC):
                    P.op("pe", "matmul", ps[:, bg, :], wst[:, wb_i, 0, dc, :],
                         yf[:, dc, tt * 512:(tt + 1) * 512],
                         start=(dc == 0), stop=(dc == DC - 1),
                         _inc=(PE if dc == DC - 1 else None))
                P.wait("pe", DV, upps_war[bu])
                for dc in range(DC):
                    P.op("pe", "matmul", ps[:, bu, :], wst[:, wb_i, 1, dc, :],
                         yf[:, dc, tt * 512:(tt + 1) * 512],
                         start=(dc == 0), stop=(dc == DC - 1),
                         _inc=(PE if dc == DC - 1 else None))
                up_pe[wb_i] = PE.n
                P.wait("act", PE, up_pe[wb_i] - 1)
                P.wait("act", DV, t1dv_war[tt])
                P.op("act", "activation", t1[:, tt, :], ps[:, bg, :], AF.Silu, _inc=AC)
                t1_war2 = AC.n
                P.wait("dve", AC, t1_war2)
                P.wait("dve", DV, cw_war[tt])
                P.op("dve", "tensor_mul", t1[:, tt, :], t1[:, tt, :], cwsb[:, tt, :],
                     _inc=DV)
                P.wait("dve", DV)
                P.wait("dve", PE, up_pe[wb_i])
                P.op("dve", "tensor_mul", a_e[:, i, tt * 512:(tt + 1) * 512],
                     t1[:, tt, :], ps[:, bu, :], _inc=DV)
                upps_war[bu] = DV.n
                t1_war[tt] = AC.n
                t1dv_war[tt] = DV.n
        ae_dv = DV.n
        for i in range(ICN):
            if dw_tok[i] is not None:
                P.wait("pool", PE, dw_pe)
            dw_tok[i] = QG.start(dwst[:, i, :], edownT_in[e, i * 128:(i + 1) * 128, :])
        for i in range(ICN):
            QG.wait("pe", dw_tok[i])
        P.wait("pe", DV, ae_dv)
        nb = 0
        for dq in range(4):
            for ts in range(8):
                bank = 5 + nb % 3
                nb += 1
                P.wait("pe", DV, dn_war[bank - 5])
                for i in range(ICN):
                    P.op("pe", "matmul", ps[:, bank, :],
                         a_e[:, i, ts * 128:(ts + 1) * 128],
                         dwst[:, i, dq * 512:(dq + 1) * 512],
                         start=(i == 0), stop=(i == ICN - 1),
                         _inc=(PE if i == ICN - 1 else None))
                P.wait("dve", PE)
                P.op("dve", "tensor_add", out_acc[:, ts, dq * 512:(dq + 1) * 512],
                     ps[:, bank, :], out_acc[:, ts, dq * 512:(dq + 1) * 512], _inc=DV)
                dn_war[bank - 5] = DV.n
        dw_pe = PE.n
    scope_he.close()

    scope_hs = ExitStack()
    a_sh = scope_hs.enter_context(nc.sbuf_tensor("a_sh", [128, SICN, TC], BF16))
    sdw = scope_hs.enter_context(nc.sbuf_tensor("sdw", [128, 2, 2, 512], BF16))
    P.wait("pool", PE)
    P.wait("pool", DV)
    for i in range(SICN):
        wb_i = i % 2
        if i >= 2:
            P.wait("pool", PE, up_pe[wb_i])
        up_tok[wb_i] = QG.start(
            wst[:, wb_i, 0, :, :],
            supT_in[:, i * 128:(i + 1) * 128].rearrange("(c p) o -> p c o", p=128))
        tok_u = QG.start(
            wst[:, wb_i, 1, :, :],
            supT_in[:, SI + i * 128:SI + (i + 1) * 128].rearrange("(c p) o -> p c o", p=128))
        QG.wait("pe", up_tok[wb_i])
        QG.wait("pe", tok_u)
        for tt in range(2):
            bg = tt * 2
            bu = tt * 2 + 1
            P.wait("pe", AC, t1_war[tt])
            for dc in range(DC):
                P.op("pe", "matmul", ps[:, bg, :], wst[:, wb_i, 0, dc, :],
                     yf[:, dc, tt * 512:(tt + 1) * 512],
                     start=(dc == 0), stop=(dc == DC - 1),
                     _inc=(PE if dc == DC - 1 else None))
            P.wait("pe", DV, upps_war[bu])
            for dc in range(DC):
                P.op("pe", "matmul", ps[:, bu, :], wst[:, wb_i, 1, dc, :],
                     yf[:, dc, tt * 512:(tt + 1) * 512],
                     start=(dc == 0), stop=(dc == DC - 1),
                     _inc=(PE if dc == DC - 1 else None))
            up_pe[wb_i] = PE.n
            P.wait("act", PE, up_pe[wb_i] - 1)
            P.wait("act", DV, t1dv_war[tt])
            P.op("act", "activation", t1[:, tt, :], ps[:, bg, :], AF.Silu, _inc=AC)
            t1_war[tt] = AC.n
            P.wait("dve", AC, t1_war[tt])
            P.wait("dve", PE, up_pe[wb_i])
            P.op("dve", "tensor_mul", a_sh[:, i, tt * 512:(tt + 1) * 512],
                 t1[:, tt, :], ps[:, bu, :], _inc=DV)
            upps_war[bu] = DV.n
            t1dv_war[tt] = DV.n
    ash_dv = DV.n
    P.wait("pe", DV, ash_dv)
    for x in range(3):
        P.wait("pe", DV, dn_war[x])
    sd_tok = [None, None]
    P.wait("pool", PE)
    for dq in range(4):
        for ip in range(ICN):
            wb_i = ip % 2
            if sd_tok[wb_i] is not None:
                P.wait("pool", PE, sd_pe[wb_i])
            ta = QG.start(sdw[:, wb_i, 0, :],
                          sdownT_in[2 * ip * 128:(2 * ip + 1) * 128,
                                    dq * 512:(dq + 1) * 512])
            tb = QG.start(sdw[:, wb_i, 1, :],
                          sdownT_in[(2 * ip + 1) * 128:(2 * ip + 2) * 128,
                                    dq * 512:(dq + 1) * 512])
            sd_tok[wb_i] = tb
            QG.wait("pe", ta)
            QG.wait("pe", tb)
            if dq == 0 and ip == 0:
                sd_pe = [0, 0]
            if ip == 0:
                P.wait("pe", DV, sdacc_war if dq > 0 else 0)
            for ts in range(8):
                for half in range(2):
                    P.op("pe", "matmul", ps[:, ts, :],
                         a_sh[:, 2 * ip + half, ts * 128:(ts + 1) * 128],
                         sdw[:, wb_i, half, :],
                         start=(ip == 0 and half == 0),
                         stop=(ip == ICN - 1 and half == 1),
                         _inc=(PE if (ts, half) == (7, 1) else None))
            sd_pe[wb_i] = PE.n
        P.wait("dve", PE)
        for ts in range(8):
            P.op("dve", "tensor_add", out_acc[:, ts, dq * 512:(dq + 1) * 512],
                 ps[:, ts, :], out_acc[:, ts, dq * 512:(dq + 1) * 512],
                 _inc=(DV if ts == 7 else None))
        sdacc_war = DV.n
    scope_hs.close()
    scope_h.close()

    P.wait("sync", DV)
    for r in range(8):
        QS.start(out_hbm.rearrange("(r p) d -> r p d", p=128)[r, :, :], out_acc[:, r, :])
    QS.drain("sync")
    QG.drain("sync")

    with nc.Block() as block:
        P.emit(block)
    ctx.close()
    return nc, dbgs


def _prep_inputs(inputs):
    x = np.asarray(inputs["L0_attn_hidden_states"]).reshape(8192, D)
    wq = np.asarray(inputs["L0_attn_Wq"])
    wk = np.asarray(inputs["L0_attn_Wk"])
    wv = np.asarray(inputs["L0_attn_Wv"])
    wb = np.asarray(inputs["L0_attn_Wb"])
    cq = np.asarray(inputs["L0_attn_conv_q"])
    ck = np.asarray(inputs["L0_attn_conv_k"])
    cv = np.asarray(inputs["L0_attn_conv_v"])
    nw = np.asarray(inputs["L0_attn_norm_w"]).reshape(1, DK)
    wo = np.asarray(inputs["L0_attn_Wo"])
    gw = np.asarray(inputs["L0_ffn_gate_w"])
    eu = np.asarray(inputs["L0_ffn_expert_up_w"])
    ed = np.asarray(inputs["L0_ffn_expert_down_w"])
    su = np.asarray(inputs["L0_ffn_shared_up_w"])
    sd = np.asarray(inputs["L0_ffn_shared_down_w"])

    convw = np.stack([cq, ck, cv], axis=1)
    convw = convw.reshape(DC, 128, 3, KCONV).transpose(1, 2, 0, 3)
    convw = np.ascontiguousarray(convw, dtype=np.float32)

    gatewT = np.ascontiguousarray(gw.T, dtype=np.float32)
    eupT = np.ascontiguousarray(eu.transpose(0, 2, 1), dtype=np.float32)
    edownT = np.ascontiguousarray(ed.transpose(0, 2, 1), dtype=np.float32)
    supT = np.ascontiguousarray(su.T, dtype=np.float32)
    sdownT = np.ascontiguousarray(sd.T, dtype=np.float32)

    ident = np.eye(128, dtype=np.float32)
    masks = np.stack([np.tril(np.ones((128, 128)), -1),
                      np.triu(np.ones((128, 128)), 1),
                      np.triu(np.ones((128, 128)), 0)], axis=1).astype(np.float32)
    i8 = np.tile(ident, (1, H)).astype(np.float32)
    headsel = np.zeros((128, DC, H), dtype=np.float32)
    for c in range(DC):
        headsel[:, c, c // 2] = 1.0
    onehot8 = np.zeros((H, H * 128), dtype=np.float32)
    for h in range(H):
        onehot8[h, h * 128:(h + 1) * 128] = 1.0
    onehotE = np.zeros((NE, NE * 128), dtype=np.float32)
    for e in range(NE):
        onehotE[e, e * 128:(e + 1) * 128] = 1.0

    shared = dict(wq=np.ascontiguousarray(wq, dtype=np.float32),
                  wk=np.ascontiguousarray(wk, dtype=np.float32),
                  wv=np.ascontiguousarray(wv, dtype=np.float32),
                  wb=np.ascontiguousarray(wb, dtype=np.float32),
                  convw=convw, normw=np.ascontiguousarray(nw, dtype=np.float32),
                  wo=np.ascontiguousarray(wo, dtype=np.float32),
                  gatewT=gatewT, eupT=eupT, edownT=edownT, supT=supT, sdownT=sdownT,
                  ident=ident, masks=masks, i8=i8, headsel=headsel,
                  onehot8=onehot8, onehotE=onehotE)

    in_maps = []
    for i in range(NCORE):
        m = dict(shared)
        m["x"] = np.ascontiguousarray(x[i * TC:(i + 1) * TC], dtype=np.float32)
        m["xh"] = (np.zeros((3, D), dtype=np.float32) if i == 0 else
                   np.ascontiguousarray(x[i * TC - 3:i * TC], dtype=np.float32))
        selcol = np.zeros((128, NCORE), dtype=np.float32)
        selcol[:, i] = 1.0
        m["selcol"] = selcol
        in_maps.append(m)
    return in_maps


def kernel(**inputs):
    keys = tuple(DBG.get("keys", ()))
    nc, dbgs = build_nc(debug_keys=keys, phase=DBG.get("phase", "H"), esteps=DBG.get("esteps", NCORE - 1))
    in_maps = _prep_inputs(inputs)
    res = run_bass_kernel_spmd(nc, in_maps, core_ids=list(range(NCORE)))
    if keys:
        DBG["out"] = [dict(r) for r in res.results]
    out = np.concatenate([res.results[i]["out"] for i in range(NCORE)], axis=0)
    return out.reshape(1, 8192, D)


# version 03
# speedup vs baseline: 1.0052x; 1.0052x over previous
import numpy as np
import ml_dtypes
import concourse.bass as bass
import concourse.mybir as mybir
from concourse.bass_utils import run_bass_kernel_spmd

F32 = mybir.dt.float32
BF16 = mybir.dt.bfloat16
AF = mybir.ActivationFunctionType
ALU = mybir.AluOpType
AX = mybir.AxisListType

NCORE = 8
TC = 1024
D = 2048
DC = 16
H = 8
DK = 256
CH = 128
NCH = TC // CH
KCONV = 4
L2_EPS = 1e-6
RMS_EPS = 1e-5
LN_EPS = 1e-5
NE = 16
IE = 1408
ICN = IE // 128
SI = 2816
SICN = SI // 128

PTILES = [(0, 343), (343, 343), (686, 341)]

DBG = {"keys": (), "out": None, "phase": "H"}


class Ctr:
    __slots__ = ("h", "n", "name")

    def __init__(self, h, name):
        self.h, self.n, self.name = h, 0, name


class Prog:
    ENG = ("pe", "dve", "act", "pool", "sync")

    def __init__(self):
        self.ops = {e: [] for e in self.ENG}

    def op(self, e, meth, *a, _inc=None, _amt=1, **k):
        def f(eng, meth=meth, a=a, k=k, _inc=_inc, _amt=_amt):
            ins = getattr(eng, meth)(*a, **k)
            if _inc is not None:
                ins.then_inc(_inc.h, _amt)
            return ins
        self.ops[e].append(f)
        if _inc is not None:
            _inc.n += _amt

    def wait(self, e, ctr, n=None):
        v = ctr.n if n is None else n
        if v <= 0:
            return
        self.ops[e].append(lambda eng, s=ctr.h, v=v: eng.wait_ge(s, v))

    def emit(self, block):
        def run(stream):
            def body(eng):
                for f in stream:
                    f(eng)
            return body
        block.tensor(run(self.ops["pe"]))
        block.vector(run(self.ops["dve"]))
        block.scalar(run(self.ops["act"]))
        block.gpsimd(run(self.ops["pool"]))
        block.sync(run(self.ops["sync"]))


class DmaQ:

    def __init__(self, P, qname, sems):
        self.P, self.q, self.sems = P, qname, sems
        self.uses = [0] * len(sems)
        self.i = 0

    def start(self, out, in_, **kw):
        r = self.i % len(self.sems)
        ctr = self.sems[r]
        if self.uses[r]:
            self.P.wait(self.q, ctr, 16 * self.uses[r])
        self.P.op(self.q, "dma_start", out, in_, _inc=ctr, _amt=16, **kw)
        self.uses[r] += 1
        self.i += 1
        return (ctr, 16 * self.uses[r])

    def wait(self, e, tok):
        self.P.wait(e, tok[0], tok[1])

    def drain(self, e):
        for r, ctr in enumerate(self.sems):
            if self.uses[r]:
                self.P.wait(e, ctr, 16 * self.uses[r])


def build_nc(debug_keys=(), phase="H", esteps=NCORE - 1):
    nc = bass.Bass("TRN2", target_bir_lowering=False)
    dbgs = {}

    def din(name, shape, dtype=F32):
        return nc.dram_tensor(name, shape, dtype, kind="ExternalInput").ap()

    x_in = din("x", [TC, D])
    xh_in = din("xh", [3, D])
    wq_in = din("wq", [D, D])
    wk_in = din("wk", [D, D])
    wv_in = din("wv", [D, D])
    wb_in = din("wb", [D, H])
    convw_in = din("convw", [128, 3, DC, KCONV])
    normw_in = din("normw", [1, DK])
    wo_in = din("wo", [D, D])
    gatewT_in = din("gatewT", [D, NE])
    eupT_in = din("eupT", [NE, D, 2 * IE])
    edownT_in = din("edownT", [NE, IE, D])
    supT_in = din("supT", [D, 2 * SI])
    sdownT_in = din("sdownT", [SI, D])
    ident_in = din("ident", [128, 128])
    masks_in = din("masks", [128, 3, 128])
    i8_in = din("i8", [128, H * 128])
    headsel_in = din("headsel", [128, DC, H])
    onehot8_in = din("onehot8", [H, H * 128])
    onehotE_in = din("onehotE", [NE, NE * 128])
    selcol_in = din("selcol", [128, NCORE])
    out_hbm = nc.dram_tensor("out", [TC, D], F32, kind="ExternalOutput").ap()

    SPW = 896
    spill = nc.dram_tensor("spill", [NCH * H, 128, SPW], BF16, kind="Internal").ap()
    ag_in = nc.dram_tensor("ag_in", [H * DK, 2 * DK], BF16, kind="Internal").ap()
    ag_out = nc.dram_tensor("ag_out", [NCORE * H * DK, 2 * DK], BF16, kind="Internal").ap()

    P = Prog()
    from contextlib import ExitStack
    ctx = ExitStack()

    def sbuf(name, shape, dtype):
        return ctx.enter_context(nc.sbuf_tensor(name, shape, dtype))

    def sem(name):
        return Ctr(ctx.enter_context(nc.semaphore(name)), name)

    PE = sem("c_pe")
    DV = sem("c_dve")
    AC = sem("c_act")
    CC = sem("c_cc")
    QG = DmaQ(P, "pool", [sem(f"c_g{i}") for i in range(4)])
    QS = DmaQ(P, "sync", [sem(f"c_s{i}") for i in range(4)])

    ident_bf = sbuf("ident_bf", [128, 128], BF16)
    ident_f = sbuf("ident_f", [128, 128], F32)
    i8_bf = sbuf("i8_bf", [128, H, 128], BF16)
    masks_f = sbuf("masks_f", [128, 3, 128], F32)
    convw_sb = sbuf("convw_sb", [128, 3, DC, KCONV], F32)
    headsel_bf = sbuf("headsel_bf", [128, DC, H], BF16)
    selcol_f = sbuf("selcol_f", [128, NCORE], F32)
    normw_f = sbuf("normw_f", [1, DK], F32)
    normw_bc = sbuf("normw_bc", [128, DK], F32)
    gatewT_bf = sbuf("gatewT_bf", [128, DC, NE], BF16)
    wb_bf = sbuf("wb_bf", [128, DC, H], BF16)
    ones_f = sbuf("ones_f", [1, 128], F32)
    consts = sbuf("consts", [128, 8], F32)
    yf = sbuf("yf", [128, DC, 1028], BF16)
    beta_fm = sbuf("beta_fm", [8, 1027], F32)
    beta_col = sbuf("beta_col", [128, NCH, H], F32)
    ps = ctx.enter_context(nc.psum_tensor("ps", [128, 8, 512], F32))
    psb = ps.ap().bitcast(BF16)

    CONSTS = {}

    def rconst(val):
        val = float(val)
        if val not in CONSTS:
            i = len(CONSTS)
            assert i < 8
            CONSTS[val] = i
            P.op("dve", "memset", consts[:, i:i + 1], val, _inc=DV)
            nc.const_aps.register(F32, val, consts[:, i:i + 1])
        return val

    for dst, srcap in ((ident_f[:, :], ident_in), (masks_f[:, :, :], masks_in),
                       (convw_sb[:, :, :, :], convw_in), (selcol_f[:, :], selcol_in),
                       (normw_f[:, :], normw_in)):
        QS.start(dst, srcap)
    for dst, srcap in ((ident_bf[:, :], ident_in),
                       (i8_bf[:, :, :], i8_in.rearrange("p (h x) -> p h x", h=H)),
                       (headsel_bf[:, :, :], headsel_in),
                       (gatewT_bf[:, :, :], gatewT_in.rearrange("(c p) e -> p c e", p=128)),
                       (wb_bf[:, :, :], wb_in.rearrange("(c p) h -> p c h", p=128))):
        QG.start(dst, srcap)
    P.op("dve", "memset", ones_f[:, :], 1.0, _inc=DV)
    P.op("dve", "memset", beta_fm[0:8, 0:3], 0.0, _inc=DV)
    P.op("dve", "memset", yf[:, :, 1027:1028], 0.0, _inc=DV)
    rconst(LN_EPS)
    QS.drain("pe")
    QG.drain("pe")
    P.wait("pe", DV)
    P.op("pe", "matmul", ps[:, 0, 0:DK], ones_f[:, :], normw_f[:, :],
         start=True, stop=True, _inc=PE)
    P.wait("dve", PE)
    P.op("dve", "tensor_copy", normw_bc[:, :], ps[:, 0, 0:DK], _inc=DV)
    QS.drain("dve"); QS.drain("act"); QG.drain("dve"); QG.drain("act")
    A_done = dict(pe=PE.n, dve=DV.n, act=AC.n)

    x_rows = x_in.rearrange("(r p) d -> r p d", p=128)
    scope_b1 = ExitStack()
    xrow = scope_b1.enter_context(nc.sbuf_tensor("xrow", [128, 2, D], F32))
    ytm = scope_b1.enter_context(nc.sbuf_tensor("ytm", [128, 2, D], BF16))
    lnst = scope_b1.enter_context(nc.sbuf_tensor("lnst", [128, 8], F32))
    war_x = [0, 0]
    war_ytm = [0, 0]
    tr_evac = [0, 0]
    for r in range(9):
        buf = r % 2
        nrow = 128 if r < 8 else 3
        src = x_rows[r, :, :] if r < 8 else xh_in
        P.wait("sync", DV, war_x[buf])
        tok = QS.start(xrow[0:nrow, buf, :], src)
        QS.wait("act", tok)
        P.wait("act", PE, war_ytm[buf])
        P.op("act", "activation", ytm[0:nrow, buf, :], xrow[0:nrow, buf, :], AF.Identity,
             accum_out=lnst[0:nrow, 0 + buf:1 + buf], _inc=AC)
        P.wait("act", AC)
        P.op("act", "activation", ytm[0:nrow, buf, :], xrow[0:nrow, buf, :], AF.Square,
             accum_out=lnst[0:nrow, 2 + buf:3 + buf], _inc=AC)
        P.wait("dve", AC)
        c0, c2, c4, c6 = (0 + buf, 2 + buf, 4 + buf, 6 + buf)
        P.op("dve", "tensor_scalar", lnst[0:nrow, c0:c0 + 1], lnst[0:nrow, c0:c0 + 1],
             1.0 / D, None, ALU.mult, _inc=DV)
        P.op("dve", "tensor_scalar", lnst[0:nrow, c2:c2 + 1], lnst[0:nrow, c2:c2 + 1],
             1.0 / D, None, ALU.mult, _inc=DV)
        P.wait("dve", DV)
        P.op("dve", "tensor_mul", lnst[0:nrow, c4:c4 + 1], lnst[0:nrow, c0:c0 + 1],
             lnst[0:nrow, c0:c0 + 1], _inc=DV)
        P.wait("dve", DV)
        P.op("dve", "tensor_sub", lnst[0:nrow, c2:c2 + 1], lnst[0:nrow, c2:c2 + 1],
             lnst[0:nrow, c4:c4 + 1], _inc=DV)
        P.wait("act", DV)
        P.op("act", "activation", lnst[0:nrow, c4:c4 + 1], lnst[0:nrow, c2:c2 + 1],
             AF.Sqrt, bias=rconst(LN_EPS), _inc=AC)
        P.wait("dve", AC)
        P.op("dve", "reciprocal", lnst[0:nrow, c6:c6 + 1], lnst[0:nrow, c4:c4 + 1], _inc=DV)
        P.wait("dve", DV)
        P.op("dve", "tensor_scalar", ytm[0:nrow, buf, :], xrow[0:nrow, buf, :],
             lnst[0:nrow, c0:c0 + 1], lnst[0:nrow, c6:c6 + 1],
             ALU.subtract, ALU.mult, _inc=DV)
        war_x[buf] = DV.n
        col0 = 3 + r * 128 if r < 8 else 0
        P.wait("pe", DV)
        for c in range(DC):
            bank = c % 2
            P.wait("pe", AC, tr_evac[bank])
            idb = ident_bf[:, :] if nrow == 128 else ident_bf[0:3, 0:3]
            P.op("pe", "transpose", psb[:, 6 + bank, 0:nrow],
                 ytm[0:nrow, buf, c * 128:(c + 1) * 128], idb, _inc=PE)
            P.wait("act", PE)
            P.op("act", "copy", yf[:, c, col0:col0 + nrow], psb[:, 6 + bank, 0:nrow],
                 _inc=AC)
            tr_evac[bank] = AC.n
        war_ytm[buf] = PE.n
    scope_b1.close()
    yf_done = AC.n

    scope_qkv = ExitStack()

    def sbq(name, shape, dtype):
        return scope_qkv.enter_context(nc.sbuf_tensor(name, shape, dtype))

    q_bf = sbq("q_bf", [128, DC, TC], BF16)
    k_bf = sbq("k_bf", [128, DC, TC], BF16)
    v_bf = sbq("v_bf", [128, DC, TC], BF16)

    scope_b2 = ExitStack()
    wqt = scope_b2.enter_context(nc.sbuf_tensor("wqt", [128, 2, DC, 512], BF16))
    qext = scope_b2.enter_context(nc.sbuf_tensor("qext", [128, 4, 348], F32))
    cacc = scope_b2.enter_context(nc.sbuf_tensor("cacc", [128, 4, 344], F32))
    sqb = scope_b2.enter_context(nc.sbuf_tensor("sqb", [128, 4, 344], BF16))
    tails = scope_b2.enter_context(nc.sbuf_tensor("tails", [128, 4, 3], F32))
    rss = scope_b2.enter_context(nc.sbuf_tensor("rss", [8, 4, 344], F32))
    onehot8_f = scope_b2.enter_context(nc.sbuf_tensor("onehot8_f", [H, H * 128], F32))

    rconst(256.0 * L2_EPS)
    rconst(1.0 * L2_EPS)
    projs = [("q", wq_in, q_bf, 0, 256.0), ("k", wk_in, k_bf, 1, 1.0),
             ("v", wv_in, v_bf, 2, 0.0)]
    war_w = {0: 0, 1: 0}
    pr_evac = [0, 0]
    rb_evac = [0, 0]
    war_qext = 0
    oh8_tok = QS.start(onehot8_f[:, :], onehot8_in)
    QS.wait("pe", oh8_tok)
    for pj, (pname, w_in, obuf, pidx, l2s) in enumerate(projs):
        w_re = w_in.rearrange("(c p) o -> p c o", p=128)
        for qtr in range(4):
            gq = pj * 4 + qtr
            wb_i = gq % 2
            P.wait("pool", PE, war_w[wb_i])
            tok = QG.start(wqt[:, wb_i, :, :], w_re[:, :, qtr * 512:(qtr + 1) * 512])
            QG.wait("pe", tok)
            if gq == 0:
                P.wait("pe", AC, yf_done)
            for ti, (toff, tn) in enumerate(PTILES):
                lo = 3 if ti == 0 else 0
                tok0 = toff - 3 + lo
                nv = tn - lo
                for oci in range(4):
                    bank = oci % 2
                    P.wait("pe", DV, war_qext)
                    P.wait("act", DV, war_qext)
                    P.wait("pe", AC, pr_evac[bank])
                    for dc in range(DC):
                        P.op("pe", "matmul", ps[:, bank, 0:tn],
                             wqt[:, wb_i, dc, oci * 128:(oci + 1) * 128],
                             yf[:, dc, toff:toff + tn],
                             start=(dc == 0), stop=(dc == DC - 1),
                             _inc=(PE if dc == DC - 1 else None))
                    P.wait("act", PE)
                    P.op("act", "copy", qext[:, oci, 3:3 + tn], ps[:, bank, 0:tn],
                         _inc=AC)
                    pr_evac[bank] = AC.n
                P.wait("dve", AC)
                if ti > 0:
                    P.op("dve", "tensor_copy", qext[:, :, 0:3], tails[:, :, :], _inc=DV)
                else:
                    P.op("dve", "memset", qext[:, :, 0:3], 0.0, _inc=DV)
                P.wait("dve", DV)
                cw = convw_sb[:, pidx, qtr * 4:(qtr + 1) * 4, :]
                P.op("dve", "tensor_tensor", cacc[:, :, 0:tn], qext[:, :, 0:tn],
                     cw[:, :, 0:1].broadcast_to((128, 4, tn)), ALU.mult, _inc=DV)
                for i in range(1, KCONV):
                    P.wait("dve", DV)
                    for ci in range(4):
                        P.op("dve", "scalar_tensor_tensor", cacc[:, ci, 0:tn],
                             qext[:, ci, i:i + tn], cw[:, ci, i:i + 1],
                             cacc[:, ci, 0:tn], ALU.mult, ALU.add,
                             _inc=(DV if ci == 3 else None))
                P.wait("dve", DV)
                P.op("dve", "tensor_copy", tails[:, :, :], qext[:, :, tn:tn + 3], _inc=DV)
                war_qext = DV.n
                P.wait("act", DV)
                if l2s == 0.0:
                    P.op("act", "activation", obuf[:, qtr * 4:(qtr + 1) * 4, tok0:tok0 + nv],
                         cacc[:, :, lo:tn], AF.Silu, _inc=AC)
                    war_qext = DV.n
                else:
                    sil = qext[:, :, 4:348]
                    P.op("act", "activation", sil[:, :, lo:tn], cacc[:, :, lo:tn], AF.Silu,
                         _inc=AC)
                    P.wait("act", AC)
                    P.op("act", "activation", sqb[:, :, lo:tn], sil[:, :, lo:tn], AF.Square,
                         _inc=AC)
                    P.wait("pe", AC)
                    for ci in range(4):
                        dc = qtr * 4 + ci
                        P.op("pe", "matmul", ps[0:8, 4, lo:tn], headsel_bf[:, dc, :],
                             sqb[:, ci, lo:tn], start=(ci == 0), stop=(ci == 3),
                             _inc=(PE if ci == 3 else None))
                    P.wait("act", PE)
                    P.op("act", "activation", rss[0:8, 0, lo:tn], ps[0:8, 4, lo:tn],
                         AF.Sqrt, bias=rconst(l2s * L2_EPS), scale=rconst(l2s), _inc=AC)
                    P.wait("dve", AC)
                    P.op("dve", "reciprocal", rss[0:8, 1, lo:tn], rss[0:8, 0, lo:tn],
                         _inc=DV)
                    P.wait("pe", DV)
                    for hh in range(2):
                        h = qtr * 2 + hh
                        bank = 5 + hh
                        P.wait("pe", DV, rb_evac[hh])
                        P.op("pe", "matmul", ps[:, bank, lo:tn],
                             onehot8_f[:, h * 128:(h + 1) * 128], rss[0:8, 1, lo:tn],
                             start=True, stop=True, _inc=PE)
                        P.wait("dve", PE)
                        for cc_ in range(2):
                            dc = qtr * 4 + hh * 2 + cc_
                            P.op("dve", "tensor_mul", obuf[:, dc, tok0:tok0 + nv],
                                 sil[:, hh * 2 + cc_, lo:tn], ps[:, bank, lo:tn],
                                 _inc=(DV if cc_ == 1 else None))
                        rb_evac[hh] = DV.n
                    war_qext = DV.n
            war_w[wb_i] = PE.n
            if gq == 0:
                pass
    for ti, (toff, tn) in enumerate(PTILES):
        lo = 3 if ti == 0 else 0
        P.wait("pe", AC)
        for dc in range(DC):
            P.op("pe", "matmul", ps[0:8, 4, 0:tn], wb_bf[:, dc, :],
                 yf[:, dc, toff:toff + tn], start=(dc == 0), stop=(dc == DC - 1),
                 _inc=(PE if dc == DC - 1 else None))
        P.wait("act", PE)
        P.op("act", "activation", beta_fm[0:8, toff + lo:toff + tn], ps[0:8, 4, lo:tn],
             AF.Sigmoid, _inc=AC)
    scope_b2.close()
    B_done = dict(pe=PE.n, dve=DV.n, act=AC.n)

    def dump(key, src_sb, shape, deps=()):
        if key not in debug_keys:
            return
        t = nc.dram_tensor("dbg_" + key, shape, F32, kind="ExternalOutput").ap()
        dbgs[key] = t
        P.wait("pool", DV)
        P.wait("pool", AC)
        P.wait("pool", PE)
        tok = QG.start(t, src_sb)
        for e in ("pe", "dve", "act", "sync"):
            QG.wait(e, tok)

    dump("q", q_bf[:, :, :], [128, DC, TC])
    dump("k", k_bf[:, :, :], [128, DC, TC])
    dump("v", v_bf[:, :, :], [128, DC, TC])
    dump("beta", beta_fm[0:8, :], [8, 1027])
    dump("yfd", yf[:, :, :], [128, DC, 1028])

    def finish_early():
        P.wait("sync", AC)
        P.wait("sync", DV)
        P.wait("sync", PE)
        zer2 = sbuf("zer2", [128, 16], F32)
        P.op("dve", "memset", zer2[:, :], 0.0, _inc=DV)
        P.wait("sync", DV)
        QS.start(out_hbm[0:128, 0:16], zer2[:, :])
        QS.drain("sync")
        QG.drain("sync")
        with nc.Block() as block:
            P.emit(block)
        ctx.close()
        return nc, dbgs

    if phase == "B":
        return finish_early()

    scope_c = ExitStack()

    def sbc(name, shape, dtype):
        return scope_c.enter_context(nc.sbuf_tensor(name, shape, dtype))

    PD_f = sbc("PD_f", [128, H, 2, 512], F32)
    PD_bf = sbc("PD_bf", [128, 2, 512], BF16)
    N8 = sbc("N8", [128, H, 128], BF16)
    M8 = sbc("M8", [128, H, 128], BF16)
    X8 = sbc("X8", [128, H, 128], BF16)
    Y8 = sbc("Y8", [128, H, 128], BF16)
    G8 = sbc("G8", [128, H, 128], BF16)
    H8 = sbc("H8", [128, H, 128], BF16)
    TT8 = sbc("TT8", [128, H, 128], BF16)
    Gm8 = sbc("Gm8", [128, H, 128], BF16)
    KT = sbc("KT", [128, 2, 128], BF16)
    VT = sbc("VT", [128, 256], BF16)
    XE = sbc("XE", [128, 512], BF16)
    UE = sbc("UE", [128, 512], BF16)
    nbU = sbc("nbU", [128, 128], F32)
    brow_s = sbc("brow_s", [1, 128], F32)
    stage = sbc("stage", [128, 2, SPW], BF16)

    P.wait("dve", AC, B_done["act"])
    P.op("dve", "memset", PD_f[:, :, :, :], 0.0, _inc=DV)
    P.wait("dve", DV)
    for h in range(H):
        for j in range(2):
            P.op("dve", "tensor_copy", PD_f[:, h, j, j * 128:(j + 1) * 128],
                 ident_f[:, :], _inc=(DV if (h, j) == (H - 1, 1) else None))
    pd_init_dv = DV.n

    stg_tok = [None, None]
    for c in range(NCH):
        tc0 = c * CH
        yc0 = tc0 + 3
        P.wait("pe", AC)
        P.wait("pe", DV)
        P.op("pe", "transpose", ps[0:128, 2, 0:8], beta_fm[0:8, yc0:yc0 + 128],
             ident_f[0:8, 0:8], _inc=PE)
        P.wait("dve", PE)
        P.op("dve", "tensor_copy", beta_col[:, c, :], ps[:, 2, 0:8], _inc=DV)
        bcol_dv = DV.n
        for h in range(H):
            for j in range(2):
                P.op("pe", "matmul", ps[:, h // 4, (h % 4) * 128:(h % 4 + 1) * 128],
                     k_bf[:, 2 * h + j, tc0:tc0 + CH], k_bf[:, 2 * h + j, tc0:tc0 + CH],
                     start=(j == 0), stop=(j == 1), _inc=(PE if j == 1 else None))
        a_pe = PE.n
        for h in range(H):
            P.wait("pe", DV, bcol_dv)
            P.wait("pe", DV)
            P.op("pe", "transpose", ps[0:1, 2, 128:256], beta_col[:, c, h:h + 1],
                 ident_f[:, :], _inc=PE)
            P.wait("dve", PE)
            P.op("dve", "tensor_copy", brow_s[:, :], ps[0:1, 2, 128:256], _inc=DV)
            P.wait("pe", DV)
            P.op("pe", "matmul", ps[:, 3, 0:128], ones_f[:, :], brow_s[:, :],
                 start=True, stop=True, _inc=PE)
            P.wait("dve", PE, a_pe)
            P.wait("dve", PE)
            P.op("dve", "tensor_mul", nbU[:, :], ps[:, 3, 0:128], masks_f[:, 1, :],
                 _inc=DV)
            P.wait("dve", DV)
            P.op("dve", "tensor_mul", N8[:, h, :],
                 ps[:, h // 4, (h % 4) * 128:(h % 4 + 1) * 128], nbU[:, :], _inc=DV)
            P.wait("pe", DV)
            P.wait("pe", AC, m_evac_war[h % 2] if (c, h) >= (0, 2) else 0)
            P.op("pe", "transpose", psb[:, 6 + h % 2, 0:128],
                 N8[:, h, :], ident_bf[:, :], _inc=PE)
            if (c, h) == (0, 0):
                m_evac_war = [0, 0]
            P.wait("act", PE)
            P.op("act", "copy", M8[:, h, :], psb[:, 6 + h % 2, 0:128],
                 _inc=AC)
            m_evac_war[h % 2] = AC.n
        P.wait("dve", AC)
        P.op("dve", "tensor_sub", X8[:, :, :], i8_bf[:, :, :], N8[:, :, :], _inc=DV)
        P.op("dve", "tensor_sub", Y8[:, :, :], i8_bf[:, :, :], M8[:, :, :], _inc=DV)
        xy_dv = DV.n
        gh_ac = AC.n
        for lv in range(1, 7):
            gsrc, hsrc = (N8, M8) if lv == 1 else (G8, H8)
            P.wait("pe", AC, gh_ac)
            P.wait("pe", DV, xy_dv)
            for h in range(H):
                P.op("pe", "matmul", ps[:, 4 + h // 4, (h % 4) * 128:(h % 4 + 1) * 128],
                     hsrc[:, h, :], gsrc[:, h, :], start=True, stop=True)
                P.op("pe", "matmul", ps[:, 6 + h // 4, (h % 4) * 128:(h % 4 + 1) * 128],
                     gsrc[:, h, :], hsrc[:, h, :], start=True, stop=True,
                     _inc=(PE if h == H - 1 else None))
            P.wait("act", PE)
            P.op("act", "copy", G8[:, :, :], ps[:, 4:6, :].rearrange("p b x -> p (b x)")
                 .rearrange("p (a y) -> p a y", a=H), _inc=AC)
            P.op("act", "copy", H8[:, :, :], ps[:, 6:8, :].rearrange("p b x -> p (b x)")
                 .rearrange("p (a y) -> p a y", a=H), _inc=AC)
            gh_ac = AC.n
            P.wait("pe", AC, gh_ac)
            for h in range(H):
                P.op("pe", "matmul", ps[:, h // 4, (h % 4) * 128:(h % 4 + 1) * 128],
                     Y8[:, h, :], G8[:, h, :], start=True, stop=True)
                P.op("pe", "matmul", ps[:, 2 + h // 4, (h % 4) * 128:(h % 4 + 1) * 128],
                     G8[:, h, :], Y8[:, h, :], start=True, stop=True,
                     _inc=(PE if h == H - 1 else None))
            P.wait("dve", PE)
            P.op("dve", "tensor_add", X8[:, :, :],
                 ps[:, 0:2, :].rearrange("p b x -> p (b x)").rearrange("p (a y) -> p a y", a=H),
                 X8[:, :, :], _inc=DV)
            P.op("dve", "tensor_add", Y8[:, :, :],
                 ps[:, 2:4, :].rearrange("p b x -> p (b x)").rearrange("p (a y) -> p a y", a=H),
                 Y8[:, :, :], _inc=DV)
            xy_dv = DV.n
        P.wait("dve", DV, xy_dv)
        for h in range(H):
            P.op("dve", "tensor_scalar", TT8[:, h, :], X8[:, h, :],
                 beta_col[:, c, h:h + 1], None, ALU.mult,
                 _inc=(DV if h == H - 1 else None))
        tt_dv = DV.n
        for h in range(H):
            P.wait("pe", DV)
            for j in range(2):
                P.op("pe", "matmul", ps[:, 2 + h % 2, 0:128],
                     k_bf[:, 2 * h + j, tc0:tc0 + CH], q_bf[:, 2 * h + j, tc0:tc0 + CH],
                     start=(j == 0), stop=(j == 1), _inc=(PE if j == 1 else None))
            P.wait("dve", PE)
            P.op("dve", "tensor_mul", Gm8[:, h, :], ps[:, 2 + h % 2, 0:128],
                 masks_f[:, 2, :], _inc=DV)
        gm_dv = DV.n
        if c == 0:
            xe_war = [0, 0]
            ue_war = [0, 0]
        for h in range(H):
            sb = (c * H + h) % 2
            P.wait("pe", AC)
            for j in range(2):
                P.op("pe", "transpose", psb[:, 7, j * 128:(j + 1) * 128],
                     k_bf[:, 2 * h + j, tc0:tc0 + CH], ident_bf[:, :], _inc=PE)
                P.op("pe", "transpose", psb[:, 7, (2 + j) * 128:(3 + j) * 128],
                     v_bf[:, 2 * h + j, tc0:tc0 + CH], ident_bf[:, :], _inc=PE)
            P.wait("act", PE)
            if (c, h) != (0, 0):
                P.wait("act", PE, ktprot_pe)
                P.wait("act", DV, ktprot_dv)
            P.op("act", "copy", KT[:, 0, :], psb[:, 7, 0:128])
            P.op("act", "copy", KT[:, 1, :], psb[:, 7, 128:256])
            P.op("act", "copy", VT[:, 0:128], psb[:, 7, 256:384])
            P.op("act", "copy", VT[:, 128:256], psb[:, 7, 384:512], _inc=AC)
            ktvt_ac = AC.n
            P.wait("dve", DV, pd_init_dv)
            P.wait("dve", DV, pdupd_dv if (c, h) != (0, 0) else 0)
            P.op("dve", "tensor_copy", PD_bf[:, :, :], PD_f[:, h, :, :], _inc=DV)
            pdbf_dv = DV.n
            kb = (h % 2) * 2
            ub = kb + 1
            P.wait("pe", DV, pdbf_dv)
            P.wait("pe", DV, xe_war[h % 2])
            for j in range(2):
                P.op("pe", "matmul", ps[:, kb, :], k_bf[:, 2 * h + j, tc0:tc0 + CH],
                     PD_bf[:, j, :], start=(j == 0), stop=(j == 1),
                     _inc=(PE if j == 1 else None))
            P.wait("dve", PE)
            P.wait("dve", AC, ktvt_ac)
            P.op("dve", "tensor_scalar", XE[:, 0:256], ps[:, kb, 0:256], -1.0, None,
                 ALU.mult, _inc=DV)
            P.op("dve", "tensor_sub", XE[:, 256:512], VT[:, :], ps[:, kb, 256:512],
                 _inc=DV)
            xe_dv = DV.n
            ktprot_dv = DV.n
            P.wait("pe", DV, xe_dv)
            P.wait("pe", DV, ue_war[h % 2])
            P.op("pe", "matmul", ps[:, ub, :], TT8[:, h, :], XE[:, :],
                 start=True, stop=True, _inc=PE)
            P.wait("dve", PE)
            P.op("dve", "tensor_copy", UE[:, :], ps[:, ub, :], _inc=DV)
            ue_dv = DV.n
            ue_war[h % 2] = DV.n
            xe_war[h % 2] = DV.n
            P.wait("pe", DV, ue_dv)
            if stg_tok[sb] is not None:
                QS.wait("act", stg_tok[sb])
                QS.wait("dve", stg_tok[sb])
            for j in range(2):
                P.op("pe", "transpose", psb[:, 7, (4 + j) * 128:(5 + j) * 128],
                     UE[:, j * 128:(j + 1) * 128], ident_bf[:, :],
                     _inc=(PE if j == 1 else None))
            P.wait("act", PE)
            P.op("act", "copy", stage[:, sb, 256:384], psb[:, 7, 512:640])
            P.op("act", "copy", stage[:, sb, 384:512], psb[:, 7, 640:768], _inc=AC)
            P.wait("pe", DV, gm_dv)
            P.wait("pe", AC, qpt_war if (c, h) != (0, 0) else 0)
            for aj in range(2):
                for dj in range(2):
                    P.op("pe", "matmul", ps[:, 6, aj * 128:(aj + 1) * 128],
                         PD_bf[:, dj, aj * 128:(aj + 1) * 128],
                         q_bf[:, 2 * h + dj, tc0:tc0 + CH],
                         start=(dj == 0), stop=(dj == 1),
                         _inc=(PE if (aj, dj) == (1, 1) else None))
            P.wait("act", PE)
            P.op("act", "copy", stage[:, sb, 512:768], ps[:, 6, 0:256], _inc=AC)
            qpt_ac = AC.n
            P.wait("pe", DV, ue_dv)
            P.wait("pe", AC, qpt_ac)
            for j in range(2):
                P.op("pe", "matmul", ps[:, 6, 256:512], q_bf[:, 2 * h + j, tc0:tc0 + CH],
                     PD_bf[:, j, 256:512], start=(j == 0), stop=False)
            P.op("pe", "matmul", ps[:, 6, 256:512], Gm8[:, h, :], UE[:, 256:512],
                 start=False, stop=True, _inc=PE)
            P.wait("act", PE)
            P.op("act", "copy", stage[:, sb, 0:256], ps[:, 6, 256:512], _inc=AC)
            qpt_war = AC.n
            P.wait("dve", DV, gm_dv)
            P.op("dve", "tensor_copy", stage[:, sb, 768:896], Gm8[:, h, :], _inc=DV)
            for aj in range(2):
                P.op("pe", "matmul", ps[:, 4 + aj, :], KT[:, aj, :], UE[:, :],
                     start=True, stop=True, _inc=(PE if aj == 1 else None))
            P.wait("dve", PE)
            for aj in range(2):
                P.op("dve", "tensor_add", PD_f[:, h, aj, :], ps[:, 4 + aj, :],
                     PD_f[:, h, aj, :], _inc=(DV if aj == 1 else None))
            pdupd_dv = DV.n
            ktprot_pe = PE.n
            P.wait("sync", AC)
            P.wait("sync", DV)
            stg_tok[sb] = QS.start(spill[c * H + h, :, :], stage[:, sb, :])
    agst = stage[:, :, 0:512]
    ag_toks = []
    for h in range(H):
        P.wait("dve", DV, pdupd_dv)
        for tk in stg_tok:
            if tk is not None:
                QS.wait("dve", tk)
                QS.wait("act", tk)
        stg_tok = [None, None]
        if ag_toks:
            QS.wait("dve", ag_toks[-1])
            QS.wait("act", ag_toks[-1])
        P.op("dve", "tensor_copy", PD_bf[:, :, :], PD_f[:, h, :, :], _inc=DV)
        P.op("dve", "tensor_copy", agst[:, 0, 256:512], PD_f[:, h, 0, 256:512])
        P.op("dve", "tensor_copy", agst[:, 1, 256:512], PD_f[:, h, 1, 256:512], _inc=DV)
        P.wait("pe", DV)
        P.wait("pe", AC)
        for dj in range(2):
            for aj in range(2):
                P.op("pe", "transpose", psb[:, 7, (dj * 2 + aj) * 128:(dj * 2 + aj + 1) * 128],
                     PD_bf[:, aj, dj * 128:(dj + 1) * 128], ident_bf[:, :],
                     _inc=(PE if (dj, aj) == (1, 1) else None))
        P.wait("act", PE)
        for dj in range(2):
            for aj in range(2):
                P.op("act", "copy", agst[:, dj, aj * 128:(aj + 1) * 128],
                     psb[:, 7, (dj * 2 + aj) * 128:(dj * 2 + aj + 1) * 128],
                     _inc=(AC if (dj, aj) == (1, 1) else None))
        P.wait("sync", AC)
        P.wait("sync", DV)
        ag_toks.append(QS.start(
            ag_in[h * DK:(h + 1) * DK, :].rearrange("(j p) x -> p j x", p=128),
            agst[:, :, :]))
    scope_c.close()
    scope_qkv.close()
    C_done = dict(pe=PE.n, dve=DV.n, act=AC.n)
    if phase == "C":
        QS.drain("sync")
        return finish_early()

    for t in ag_toks:
        QS.wait("pool", t)
    P.op("pool", "collective_compute", "AllGather", ALU.bypass,
         replica_groups=[list(range(NCORE))],
         ins=[ag_in.opt()], outs=[ag_out.opt()], _inc=CC)

    out_acc = sbuf("out_acc", [128, 8, D], F32)
    scope_ef = ExitStack()
    wo_sb = scope_ef.enter_context(nc.sbuf_tensor("wo_sb", [128, DC, D], BF16))
    smine_bf = scope_ef.enter_context(nc.sbuf_tensor("smine_bf", [128, H, 2, DK], BF16))

    if phase == "D":
        P.wait("sync", CC, 1)
        return finish_early()

    P.wait("pool", CC, 1)
    wo_tok = QG.start(wo_sb[:, :, :], wo_in.rearrange("(c p) o -> p c o", p=128))

    scope_e = ExitStack()
    pdstep = scope_e.enter_context(nc.sbuf_tensor("pdstep", [128, 1, H, 2, 512], BF16))
    Sall = scope_e.enter_context(nc.sbuf_tensor("Sall", [128, H, 2, DK], BF16))

    P.wait("dve", CC, 1)
    P.op("dve", "memset", Sall[:, :, :, :], 0.0, _inc=DV)
    P.op("dve", "memset", smine_bf[:, :, :, :], 0.0, _inc=DV)
    sall_dv = DV.n
    step_tok = [None]
    pd_consumed = [0]
    chain_war = [0, 0]
    P.wait("pool", CC, 1)
    for i in range(esteps):
        pb = 0
        if step_tok[pb] is not None:
            P.wait("pool", PE, pd_consumed[pb])
        step_tok[pb] = QG.start(
            pdstep[:, pb, :, :, :],
            ag_out[i * H * DK:(i + 1) * H * DK, :].rearrange("(h j p) x -> p h j x", p=128, j=2))
        QG.wait("pe", step_tok[pb])
        if i == 0:
            P.wait("pe", DV, sall_dv)
        for h in range(H):
            bp = h % 2
            b0 = bp * 2
            P.wait("pe", DV, chain_war[bp])
            for aj in range(2):
                for dj in range(2):
                    P.op("pe", "matmul", ps[:, b0 + aj, 0:DK],
                         pdstep[:, pb, h, dj, aj * 128:(aj + 1) * 128], Sall[:, h, dj, :],
                         start=(dj == 0), stop=False)
                P.op("pe", "matmul", ps[:, b0 + aj, 0:DK], ident_bf[:, :],
                     pdstep[:, pb, h, aj, 256:512], start=False, stop=True,
                     _inc=(PE if aj == 1 else None))
            P.wait("dve", PE)
            for aj in range(2):
                P.op("dve", "scalar_tensor_tensor", smine_bf[:, h, aj, :],
                     ps[:, b0 + aj, 0:DK], selcol_f[:, i + 1:i + 2], smine_bf[:, h, aj, :],
                     ALU.mult, ALU.add, _inc=DV)
                P.op("dve", "tensor_copy", Sall[:, h, aj, :], ps[:, b0 + aj, 0:DK],
                     _inc=DV)
            chain_war[bp] = DV.n
        pd_consumed[pb] = PE.n
    P.op("dve", "engine_nop", _inc=DV)
    smine_dv = DV.n
    dump("smine", smine_bf[:, :, :, :], [128, H, 2, DK])
    scope_e.close()
    if phase == "E":
        return finish_early()

    scope_f = ExitStack()

    def sbf(name, shape, dtype):
        return scope_f.enter_context(nc.sbuf_tensor(name, shape, dtype))

    spin = sbf("spin", [128, 2, SPW], BF16)
    ofm = sbf("ofm", [128, DC, CH], BF16)
    ucb = sbf("ucb", [128, DK], BF16)
    orow = sbf("orow", [128, DK], F32)
    sq2 = sbf("sq2", [128, DK], F32)
    onst = sbf("onst", [128, 8], F32)
    xr2 = sbf("xr2", [128, 1, D], F32)
    y2tm = sbf("y2tm", [128, D], BF16)

    QS.drain("sync")
    rconst(RMS_EPS)
    P.wait("sync", DV, smine_dv)
    spin_tok = [None, None]
    x_tok = [None, None]
    ofm_war = 0
    uc_war = 0
    or_war = 0
    y2_war = 0
    for c in range(NCH):
        xb = 0
        if x_tok[xb] is not None:
            P.wait("sync", DV, x2_done)
        x_tok[xb] = QS.start(xr2[:, xb, :], x_rows[c, :, :])
        for h in range(H):
            sb = (c * H + h) % 2
            if spin_tok[sb] is not None:
                P.wait("sync", PE, sp_pe_done)
                P.wait("sync", DV, sp_dv_done)
            spin_tok[sb] = QS.start(spin[:, sb, :], spill[c * H + h, :, :])
            QS.wait("pe", spin_tok[sb])
            P.wait("pe", DV, smine_dv)
            P.wait("pe", DV, uc_war)
            for aj in range(2):
                P.op("pe", "matmul", ps[:, 0, 0:DK],
                     spin[:, sb, 256 + aj * 128:256 + (aj + 1) * 128],
                     smine_bf[:, h, aj, :], start=(aj == 0), stop=(aj == 1),
                     _inc=(PE if aj == 1 else None))
            P.wait("dve", PE)
            P.op("dve", "tensor_copy", ucb[:, :], ps[:, 0, 0:DK], _inc=DV)
            uc_dv = DV.n
            P.wait("pe", DV, uc_dv)
            P.wait("pe", DV, or_war)
            for aj in range(2):
                P.op("pe", "matmul", ps[:, 1, 0:DK],
                     spin[:, sb, 512 + aj * 128:512 + (aj + 1) * 128],
                     smine_bf[:, h, aj, :], start=(aj == 0), stop=False)
            P.op("pe", "matmul", ps[:, 1, 0:DK], spin[:, sb, 768:896], ucb[:, :],
                 start=False, stop=True, _inc=PE)
            uc_war = uc_dv
            sp_pe_done = PE.n
            P.wait("dve", PE)
            P.op("dve", "tensor_add", orow[:, :], ps[:, 1, 0:DK], spin[:, sb, 0:256],
                 _inc=DV)
            sp_dv_done = DV.n
            P.wait("act", DV)
            P.op("act", "activation", sq2[:, :], orow[:, :], AF.Square,
                 accum_out=onst[:, 0:1], _inc=AC)
            P.wait("dve", AC)
            P.op("dve", "tensor_scalar", onst[:, 1:2], onst[:, 0:1], 1.0 / DK, None,
                 ALU.mult, _inc=DV)
            P.wait("act", DV)
            P.op("act", "activation", onst[:, 2:3], onst[:, 1:2], AF.Sqrt,
                 bias=rconst(RMS_EPS), _inc=AC)
            P.wait("dve", AC)
            P.op("dve", "reciprocal", onst[:, 3:4], onst[:, 2:3], _inc=DV)
            P.wait("dve", DV)
            P.op("dve", "tensor_scalar", sq2[:, :], orow[:, :], onst[:, 3:4], None,
                 ALU.mult, _inc=DV)
            or_war = DV.n
            P.wait("dve", DV)
            P.op("dve", "tensor_mul", sq2[:, :], sq2[:, :], normw_bc[:, :], _inc=DV)
            on_dv = DV.n
            P.wait("pe", DV, on_dv)
            P.wait("pe", AC, ofm_war)
            for j in range(2):
                P.op("pe", "transpose", ps[:, 6, j * 128:(j + 1) * 128],
                     sq2[:, j * 128:(j + 1) * 128], ident_f[:, :],
                     _inc=(PE if j == 1 else None))
            P.wait("act", PE)
            for j in range(2):
                P.op("act", "copy", ofm[:, 2 * h + j, :],
                     ps[:, 6, j * 128:(j + 1) * 128],
                     _inc=(AC if j == 1 else None))
            ofm_war = AC.n
        QG.wait("pe", wo_tok)
        P.wait("pe", AC, ofm_war)
        P.wait("pe", DV, wo_war if c > 0 else 0)
        for dq in range(4):
            for kc in range(DC):
                P.op("pe", "matmul", ps[:, 2 + dq, :], ofm[:, kc, :],
                     wo_sb[:, kc, dq * 512:(dq + 1) * 512],
                     start=(kc == 0), stop=(kc == DC - 1),
                     _inc=(PE if kc == DC - 1 else None))
        QS.wait("dve", x_tok[xb])
        P.wait("dve", PE)
        for dq in range(4):
            P.op("dve", "tensor_add", out_acc[:, c, dq * 512:(dq + 1) * 512],
                 ps[:, 2 + dq, :], xr2[:, xb, dq * 512:(dq + 1) * 512],
                 _inc=(DV if dq == 3 else None))
        wo_war = DV.n
        x2_done = DV.n
        P.wait("act", DV, x2_done)
        P.wait("act", PE, y2_war)
        P.op("act", "activation", y2tm[:, :], out_acc[:, c, :], AF.Identity,
             accum_out=onst[:, 4:5], _inc=AC)
        P.wait("act", AC)
        P.op("act", "activation", y2tm[:, :], out_acc[:, c, :], AF.Square,
             accum_out=onst[:, 5:6], _inc=AC)
        P.wait("dve", AC)
        P.op("dve", "tensor_scalar", onst[:, 4:5], onst[:, 4:5], 1.0 / D, None,
             ALU.mult, _inc=DV)
        P.op("dve", "tensor_scalar", onst[:, 5:6], onst[:, 5:6], 1.0 / D, None,
             ALU.mult, _inc=DV)
        P.wait("dve", DV)
        P.op("dve", "tensor_mul", onst[:, 6:7], onst[:, 4:5], onst[:, 4:5], _inc=DV)
        P.wait("dve", DV)
        P.op("dve", "tensor_sub", onst[:, 5:6], onst[:, 5:6], onst[:, 6:7], _inc=DV)
        P.wait("act", DV)
        P.op("act", "activation", onst[:, 6:7], onst[:, 5:6], AF.Sqrt,
             bias=rconst(LN_EPS), _inc=AC)
        P.wait("dve", AC)
        P.op("dve", "reciprocal", onst[:, 7:8], onst[:, 6:7], _inc=DV)
        P.wait("dve", DV)
        P.op("dve", "tensor_scalar", y2tm[:, :], out_acc[:, c, :],
             onst[:, 4:5], onst[:, 7:8], ALU.subtract, ALU.mult, _inc=DV)
        y2_dv = DV.n
        P.wait("pe", DV, y2_dv)
        for cc2 in range(DC):
            bank = cc2 % 2
            P.wait("pe", AC, y2tr_war[bank] if c > 0 or cc2 >= 2 else 0)
            P.op("pe", "transpose", psb[:, 6 + bank, 0:128],
                 y2tm[:, cc2 * 128:(cc2 + 1) * 128], ident_bf[:, :], _inc=PE)
            if c == 0 and cc2 == 0:
                y2tr_war = [0, 0]
            P.wait("act", PE)
            P.op("act", "copy", yf[:, cc2, c * CH:(c + 1) * CH],
                 psb[:, 6 + bank, 0:128], _inc=AC)
            y2tr_war[bank] = AC.n
        y2_war = PE.n
        ofm_war = AC.n
    F_done = dict(pe=PE.n, dve=DV.n, act=AC.n)
    dump("x2", out_acc[:, :, :], [128, 8, D])
    dump("y2f", yf[:, :, 0:1024], [128, DC, 1024])
    scope_f.close()
    scope_ef.close()

    cw_fm = sbuf("cw_fm", [NE, TC], BF16)
    onehotE_bf = sbuf("onehotE_bf", [NE, NE * 128], BF16)
    scope_g = ExitStack()
    gsc = scope_g.enter_context(nc.sbuf_tensor("gsc", [128, 24, 16], F32))
    cwb = scope_g.enter_context(nc.sbuf_tensor("cwb", [128, 16], BF16))
    P.wait("pool", PE, F_done["pe"])
    P.wait("pool", DV, F_done["dve"])
    P.wait("pool", AC, F_done["act"])
    ohe_tok = QG.start(onehotE_bf[:, :], onehotE_in)

    def dvs(meth, *a, **k):
        P.wait("dve", DV)
        P.op("dve", meth, *a, _inc=DV, **k)

    SC = gsc[:, 0, :]
    SCE = gsc[:, 1, :]
    MSC = gsc[:, 2, :]
    EMK = gsc[:, 3, :]
    SMK = gsc[:, 4, :]
    PFa = gsc[:, 5, :]
    PFb = gsc[:, 6, :]
    GS = gsc[:, 7, 0:4]
    GS2 = gsc[:, 7, 4:8]
    EQ1 = gsc[:, 7, 8:12]
    EQ2 = gsc[:, 7, 12:16]
    FO1 = gsc[:, 8, 0:4]
    FO2 = gsc[:, 8, 4:8]
    GMK = gsc[:, 8, 8:12]
    PF4a = gsc[:, 9, 0:4]
    PF4b = gsc[:, 9, 4:8]
    IZ = gsc[:, 9, 8:12]
    EQT = gsc[:, 10, :]
    IZ16 = gsc[:, 11, :]
    STAT = gsc[:, 12, 0:8]

    cwtr_war = 0
    scb_ac = F_done["act"]
    scb_dv = F_done["dve"]
    for tt in range(8):
        P.wait("pe", DV, scb_dv)
        P.wait("pe", AC, scb_ac)
        P.wait("pe", AC, cwtr_war)
        for dc in range(DC):
            P.op("pe", "matmul", ps[0:128, 0, 0:NE], yf[:, dc, tt * 128:(tt + 1) * 128],
                 gatewT_bf[:, dc, :], start=(dc == 0), stop=(dc == DC - 1),
                 _inc=(PE if dc == DC - 1 else None))
        P.wait("dve", PE)
        dvs("reduce_max", STAT[:, 0:1], ps[:, 0, 0:NE], axis=AX.X)
        scb_dv = DV.n
        dvs("tensor_scalar", STAT[:, 1:2], STAT[:, 0:1], -1.0, None, ALU.mult)
        P.wait("act", DV)
        P.op("act", "activation", SCE, ps[:, 0, 0:NE], AF.Exp, bias=STAT[:, 1:2],
             accum_out=STAT[:, 2:3], _inc=AC)
        scb_ac = AC.n
        P.wait("dve", AC)
        dvs("reciprocal", STAT[:, 3:4], STAT[:, 2:3])
        dvs("tensor_scalar", SC, SCE, STAT[:, 3:4], None, ALU.mult)
        dvs("reduce_max", GS, SC.rearrange("p (g j) -> p g j", j=4), axis=AX.X)
        dvs("reduce_max", STAT[:, 4:5], GS, axis=AX.X)
        dvs("tensor_single_scalar", EQ1, GS, STAT[:, 4:5], ALU.is_equal)
        dvs("memset", PFa[:, 0:4], 0.0)
        dvs("tensor_copy", PFa[:, 1:4], EQ1[:, 0:3])
        dvs("tensor_copy", PFb[:, 0:4], PFa[:, 0:4])
        dvs("tensor_add", PFb[:, 2:4], PFa[:, 2:4], PFa[:, 0:2])
        dvs("tensor_single_scalar", IZ, PFb[:, 0:4], 0.5, ALU.is_lt)
        dvs("tensor_mul", FO1, EQ1, IZ)
        dvs("scalar_tensor_tensor", GS2, FO1, -2.0, GS, ALU.mult, ALU.add)
        dvs("reduce_max", STAT[:, 5:6], GS2, axis=AX.X)
        dvs("tensor_single_scalar", EQ2, GS2, STAT[:, 5:6], ALU.is_equal)
        dvs("memset", PFa[:, 0:4], 0.0)
        dvs("tensor_copy", PFa[:, 1:4], EQ2[:, 0:3])
        dvs("tensor_copy", PFb[:, 0:4], PFa[:, 0:4])
        dvs("tensor_add", PFb[:, 2:4], PFa[:, 2:4], PFa[:, 0:2])
        dvs("tensor_single_scalar", IZ, PFb[:, 0:4], 0.5, ALU.is_lt)
        dvs("tensor_mul", FO2, EQ2, IZ)
        dvs("tensor_add", GMK, FO1, FO2)
        dvs("tensor_copy", EMK.rearrange("p (g j) -> p g j", j=4),
            GMK.unsqueeze(2).broadcast_to((128, 4, 4)))
        dvs("tensor_mul", MSC, SC, EMK)
        dvs("memset", SMK, 0.0)
        for t4 in range(4):
            dvs("reduce_max", STAT[:, 6:7], MSC, axis=AX.X)
            dvs("tensor_single_scalar", EQT, MSC, STAT[:, 6:7], ALU.is_equal)
            dvs("memset", PFa[:, 0:16], 0.0)
            dvs("tensor_copy", PFa[:, 1:16], EQT[:, 0:15])
            dvs("tensor_copy", PFb[:, 0:16], PFa[:, 0:16])
            dvs("tensor_add", PFb[:, 2:16], PFa[:, 2:16], PFa[:, 0:14])
            dvs("tensor_copy", PFa[:, 0:16], PFb[:, 0:16])
            dvs("tensor_add", PFa[:, 4:16], PFb[:, 4:16], PFb[:, 0:12])
            dvs("tensor_copy", PFb[:, 0:16], PFa[:, 0:16])
            dvs("tensor_add", PFb[:, 8:16], PFa[:, 8:16], PFa[:, 0:8])
            dvs("tensor_single_scalar", IZ16, PFb[:, 0:16], 0.5, ALU.is_lt)
            dvs("tensor_mul", EQT, EQT, IZ16)
            dvs("tensor_add", SMK, SMK, EQT)
            dvs("scalar_tensor_tensor", MSC, EQT, -2.0, MSC, ALU.mult, ALU.add)
        dvs("tensor_mul", EMK, SC, SMK)
        dvs("reduce_sum", STAT[:, 7:8], EMK, axis=AX.X)
        dvs("tensor_scalar", STAT[:, 7:8], STAT[:, 7:8], 1e-20, None, ALU.add)
        dvs("reciprocal", STAT[:, 6:7], STAT[:, 7:8])
        dvs("tensor_scalar", cwb[:, :], EMK, STAT[:, 6:7], None, ALU.mult)
        P.wait("pe", DV)
        P.wait("pe", AC, cwtr_war if tt > 0 else F_done["act"])
        P.op("pe", "transpose", psb[0:16, 6 + tt % 2, 0:128], cwb[:, :], ident_bf[:, :],
             _inc=PE)
        P.wait("act", PE)
        P.op("act", "copy", cw_fm[0:NE, tt * 128:(tt + 1) * 128],
             psb[0:16, 6 + tt % 2, 0:128], _inc=AC)
        cwtr_war = AC.n
    scope_g.close()
    G_done = dict(pe=PE.n, dve=DV.n, act=AC.n)
    dump("cw", cw_fm[0:NE, :], [NE, TC])

    scope_h = ExitStack()

    def sbh(name, shape, dtype):
        return scope_h.enter_context(nc.sbuf_tensor(name, shape, dtype))

    wst = sbh("wst", [128, 2, 2, DC, 128], BF16)
    cwsb = sbh("cwsb", [128, 2, 512], F32)
    t1 = sbh("t1", [128, 2, 512], F32)

    scope_he = ExitStack()
    dwst = scope_he.enter_context(nc.sbuf_tensor("dwst", [128, ICN, D], BF16))
    a_e = scope_he.enter_context(nc.sbuf_tensor("a_e", [128, ICN, TC], BF16))

    P.wait("pool", PE, G_done["pe"])
    P.wait("pool", DV, G_done["dve"])
    P.wait("pool", AC, G_done["act"])
    QG.wait("pe", ohe_tok)
    P.wait("pe", DV, G_done["dve"])
    P.wait("pe", AC, G_done["act"])
    upps_war = [0, 0, 0, 0]
    t1_war = [0, 0]
    t1dv_war = [0, 0]
    up_pe = [0, 0]
    cw_war = [0, 0]
    ae_war = 0
    dn_war = [0, 0, 0]
    oa_war = G_done["dve"]
    dw_tok = [None] * ICN
    up_tok = [None, None]
    for e in range(NE):
        for i in range(ICN):
            wb_i = i % 2
            if up_tok[wb_i] is not None:
                P.wait("pool", PE, up_pe[wb_i])
            up_tok[wb_i] = QG.start(
                wst[:, wb_i, 0, :, :],
                eupT_in[e, :, i * 128:(i + 1) * 128].rearrange("(c p) o -> p c o", p=128))
            tok_u = QG.start(
                wst[:, wb_i, 1, :, :],
                eupT_in[e, :, IE + i * 128:IE + (i + 1) * 128].rearrange("(c p) o -> p c o", p=128))
            QG.wait("pe", up_tok[wb_i])
            QG.wait("pe", tok_u)
            for tt in range(2):
                if i == 0:
                    P.wait("pe", AC, G_done["act"])
                    P.wait("pe", DV, max(cw_war))
                    P.op("pe", "matmul", ps[:, 4, :], onehotE_bf[:, e * 128:(e + 1) * 128],
                         cw_fm[0:NE, tt * 512:(tt + 1) * 512], start=True, stop=True,
                         _inc=PE)
                    P.wait("dve", PE)
                    P.op("dve", "tensor_copy", cwsb[:, tt, :], ps[:, 4, :], _inc=DV)
                    cw_war[tt] = DV.n
                bg = tt * 2
                bu = tt * 2 + 1
                P.wait("pe", AC, t1_war[tt])
                for dc in range(DC):
                    P.op("pe", "matmul", ps[:, bg, :], wst[:, wb_i, 0, dc, :],
                         yf[:, dc, tt * 512:(tt + 1) * 512],
                         start=(dc == 0), stop=(dc == DC - 1),
                         _inc=(PE if dc == DC - 1 else None))
                P.wait("pe", DV, upps_war[bu])
                for dc in range(DC):
                    P.op("pe", "matmul", ps[:, bu, :], wst[:, wb_i, 1, dc, :],
                         yf[:, dc, tt * 512:(tt + 1) * 512],
                         start=(dc == 0), stop=(dc == DC - 1),
                         _inc=(PE if dc == DC - 1 else None))
                up_pe[wb_i] = PE.n
                P.wait("act", PE, up_pe[wb_i] - 1)
                P.wait("act", DV, t1dv_war[tt])
                P.op("act", "activation", t1[:, tt, :], ps[:, bg, :], AF.Silu, _inc=AC)
                t1_war2 = AC.n
                P.wait("dve", AC, t1_war2)
                P.wait("dve", DV, cw_war[tt])
                P.op("dve", "tensor_mul", t1[:, tt, :], t1[:, tt, :], cwsb[:, tt, :],
                     _inc=DV)
                P.wait("dve", DV)
                P.wait("dve", PE, up_pe[wb_i])
                P.op("dve", "tensor_mul", a_e[:, i, tt * 512:(tt + 1) * 512],
                     t1[:, tt, :], ps[:, bu, :], _inc=DV)
                upps_war[bu] = DV.n
                t1_war[tt] = AC.n
                t1dv_war[tt] = DV.n
        ae_dv = DV.n
        for i in range(ICN):
            if dw_tok[i] is not None:
                P.wait("pool", PE, dw_pe)
            dw_tok[i] = QG.start(dwst[:, i, :], edownT_in[e, i * 128:(i + 1) * 128, :])
        for i in range(ICN):
            QG.wait("pe", dw_tok[i])
        P.wait("pe", DV, ae_dv)
        nb = 0
        for dq in range(4):
            for ts in range(8):
                bank = 5 + nb % 3
                nb += 1
                P.wait("pe", DV, dn_war[bank - 5])
                for i in range(ICN):
                    P.op("pe", "matmul", ps[:, bank, :],
                         a_e[:, i, ts * 128:(ts + 1) * 128],
                         dwst[:, i, dq * 512:(dq + 1) * 512],
                         start=(i == 0), stop=(i == ICN - 1),
                         _inc=(PE if i == ICN - 1 else None))
                P.wait("dve", PE)
                P.op("dve", "tensor_add", out_acc[:, ts, dq * 512:(dq + 1) * 512],
                     ps[:, bank, :], out_acc[:, ts, dq * 512:(dq + 1) * 512], _inc=DV)
                dn_war[bank - 5] = DV.n
        dw_pe = PE.n
    scope_he.close()

    scope_hs = ExitStack()
    a_sh = scope_hs.enter_context(nc.sbuf_tensor("a_sh", [128, SICN, TC], BF16))
    sdw = scope_hs.enter_context(nc.sbuf_tensor("sdw", [128, 2, 2, 512], BF16))
    P.wait("pool", PE)
    P.wait("pool", DV)
    for i in range(SICN):
        wb_i = i % 2
        if i >= 2:
            P.wait("pool", PE, up_pe[wb_i])
        up_tok[wb_i] = QG.start(
            wst[:, wb_i, 0, :, :],
            supT_in[:, i * 128:(i + 1) * 128].rearrange("(c p) o -> p c o", p=128))
        tok_u = QG.start(
            wst[:, wb_i, 1, :, :],
            supT_in[:, SI + i * 128:SI + (i + 1) * 128].rearrange("(c p) o -> p c o", p=128))
        QG.wait("pe", up_tok[wb_i])
        QG.wait("pe", tok_u)
        for tt in range(2):
            bg = tt * 2
            bu = tt * 2 + 1
            P.wait("pe", AC, t1_war[tt])
            for dc in range(DC):
                P.op("pe", "matmul", ps[:, bg, :], wst[:, wb_i, 0, dc, :],
                     yf[:, dc, tt * 512:(tt + 1) * 512],
                     start=(dc == 0), stop=(dc == DC - 1),
                     _inc=(PE if dc == DC - 1 else None))
            P.wait("pe", DV, upps_war[bu])
            for dc in range(DC):
                P.op("pe", "matmul", ps[:, bu, :], wst[:, wb_i, 1, dc, :],
                     yf[:, dc, tt * 512:(tt + 1) * 512],
                     start=(dc == 0), stop=(dc == DC - 1),
                     _inc=(PE if dc == DC - 1 else None))
            up_pe[wb_i] = PE.n
            P.wait("act", PE, up_pe[wb_i] - 1)
            P.wait("act", DV, t1dv_war[tt])
            P.op("act", "activation", t1[:, tt, :], ps[:, bg, :], AF.Silu, _inc=AC)
            t1_war[tt] = AC.n
            P.wait("dve", AC, t1_war[tt])
            P.wait("dve", PE, up_pe[wb_i])
            P.op("dve", "tensor_mul", a_sh[:, i, tt * 512:(tt + 1) * 512],
                 t1[:, tt, :], ps[:, bu, :], _inc=DV)
            upps_war[bu] = DV.n
            t1dv_war[tt] = DV.n
    ash_dv = DV.n
    P.wait("pe", DV, ash_dv)
    for x in range(3):
        P.wait("pe", DV, dn_war[x])
    sd_tok = [None, None]
    P.wait("pool", PE)
    for dq in range(4):
        for ip in range(ICN):
            wb_i = ip % 2
            if sd_tok[wb_i] is not None:
                P.wait("pool", PE, sd_pe[wb_i])
            ta = QG.start(sdw[:, wb_i, 0, :],
                          sdownT_in[2 * ip * 128:(2 * ip + 1) * 128,
                                    dq * 512:(dq + 1) * 512])
            tb = QG.start(sdw[:, wb_i, 1, :],
                          sdownT_in[(2 * ip + 1) * 128:(2 * ip + 2) * 128,
                                    dq * 512:(dq + 1) * 512])
            sd_tok[wb_i] = tb
            QG.wait("pe", ta)
            QG.wait("pe", tb)
            if dq == 0 and ip == 0:
                sd_pe = [0, 0]
            if ip == 0:
                P.wait("pe", DV, sdacc_war if dq > 0 else 0)
            for ts in range(8):
                for half in range(2):
                    P.op("pe", "matmul", ps[:, ts, :],
                         a_sh[:, 2 * ip + half, ts * 128:(ts + 1) * 128],
                         sdw[:, wb_i, half, :],
                         start=(ip == 0 and half == 0),
                         stop=(ip == ICN - 1 and half == 1),
                         _inc=(PE if (ts, half) == (7, 1) else None))
            sd_pe[wb_i] = PE.n
        P.wait("dve", PE)
        for ts in range(8):
            P.op("dve", "tensor_add", out_acc[:, ts, dq * 512:(dq + 1) * 512],
                 ps[:, ts, :], out_acc[:, ts, dq * 512:(dq + 1) * 512],
                 _inc=(DV if ts == 7 else None))
        sdacc_war = DV.n
    scope_hs.close()
    scope_h.close()

    P.wait("sync", DV)
    for r in range(8):
        QS.start(out_hbm.rearrange("(r p) d -> r p d", p=128)[r, :, :], out_acc[:, r, :])
    QS.drain("sync")
    QG.drain("sync")

    with nc.Block() as block:
        P.emit(block)
    ctx.close()
    return nc, dbgs


def _prep_inputs(inputs):
    x = np.asarray(inputs["L0_attn_hidden_states"]).reshape(8192, D)
    wq = np.asarray(inputs["L0_attn_Wq"])
    wk = np.asarray(inputs["L0_attn_Wk"])
    wv = np.asarray(inputs["L0_attn_Wv"])
    wb = np.asarray(inputs["L0_attn_Wb"])
    cq = np.asarray(inputs["L0_attn_conv_q"])
    ck = np.asarray(inputs["L0_attn_conv_k"])
    cv = np.asarray(inputs["L0_attn_conv_v"])
    nw = np.asarray(inputs["L0_attn_norm_w"]).reshape(1, DK)
    wo = np.asarray(inputs["L0_attn_Wo"])
    gw = np.asarray(inputs["L0_ffn_gate_w"])
    eu = np.asarray(inputs["L0_ffn_expert_up_w"])
    ed = np.asarray(inputs["L0_ffn_expert_down_w"])
    su = np.asarray(inputs["L0_ffn_shared_up_w"])
    sd = np.asarray(inputs["L0_ffn_shared_down_w"])

    convw = np.stack([cq, ck, cv], axis=1)
    convw = convw.reshape(DC, 128, 3, KCONV).transpose(1, 2, 0, 3)
    convw = np.ascontiguousarray(convw, dtype=np.float32)

    gatewT = np.ascontiguousarray(gw.T, dtype=np.float32)
    eupT = np.ascontiguousarray(eu.transpose(0, 2, 1), dtype=np.float32)
    edownT = np.ascontiguousarray(ed.transpose(0, 2, 1), dtype=np.float32)
    supT = np.ascontiguousarray(su.T, dtype=np.float32)
    sdownT = np.ascontiguousarray(sd.T, dtype=np.float32)

    ident = np.eye(128, dtype=np.float32)
    masks = np.stack([np.tril(np.ones((128, 128)), -1),
                      np.triu(np.ones((128, 128)), 1),
                      np.triu(np.ones((128, 128)), 0)], axis=1).astype(np.float32)
    i8 = np.tile(ident, (1, H)).astype(np.float32)
    headsel = np.zeros((128, DC, H), dtype=np.float32)
    for c in range(DC):
        headsel[:, c, c // 2] = 1.0
    onehot8 = np.zeros((H, H * 128), dtype=np.float32)
    for h in range(H):
        onehot8[h, h * 128:(h + 1) * 128] = 1.0
    onehotE = np.zeros((NE, NE * 128), dtype=np.float32)
    for e in range(NE):
        onehotE[e, e * 128:(e + 1) * 128] = 1.0

    shared = dict(wq=np.ascontiguousarray(wq, dtype=np.float32),
                  wk=np.ascontiguousarray(wk, dtype=np.float32),
                  wv=np.ascontiguousarray(wv, dtype=np.float32),
                  wb=np.ascontiguousarray(wb, dtype=np.float32),
                  convw=convw, normw=np.ascontiguousarray(nw, dtype=np.float32),
                  wo=np.ascontiguousarray(wo, dtype=np.float32),
                  gatewT=gatewT, eupT=eupT, edownT=edownT, supT=supT, sdownT=sdownT,
                  ident=ident, masks=masks, i8=i8, headsel=headsel,
                  onehot8=onehot8, onehotE=onehotE)

    in_maps = []
    for i in range(NCORE):
        m = dict(shared)
        m["x"] = np.ascontiguousarray(x[i * TC:(i + 1) * TC], dtype=np.float32)
        m["xh"] = (np.zeros((3, D), dtype=np.float32) if i == 0 else
                   np.ascontiguousarray(x[i * TC - 3:i * TC], dtype=np.float32))
        selcol = np.zeros((128, NCORE), dtype=np.float32)
        selcol[:, i] = 1.0
        m["selcol"] = selcol
        in_maps.append(m)
    return in_maps


def kernel(**inputs):
    keys = tuple(DBG.get("keys", ()))
    nc, dbgs = build_nc(debug_keys=keys, phase=DBG.get("phase", "H"), esteps=DBG.get("esteps", NCORE - 1))
    in_maps = _prep_inputs(inputs)
    res = run_bass_kernel_spmd(nc, in_maps, core_ids=list(range(NCORE)))
    if keys:
        DBG["out"] = [dict(r) for r in res.results]
    out = np.concatenate([res.results[i]["out"] for i in range(NCORE)], axis=0)
    return out.reshape(1, 8192, D)
